# version 10
# baseline (speedup 1.0000x reference)
import numpy as np
import ml_dtypes
from contextlib import ExitStack

import concourse.bass as bass
import concourse.mybir as mybir
from concourse.bass_utils import run_bass_kernel_spmd

F32 = mybir.dt.float32
BF16 = mybir.dt.bfloat16
AF = mybir.ActivationFunctionType
ALU = mybir.AluOpType
AXX = mybir.AxisListType.X
BFNP = ml_dtypes.bfloat16

D = 1024
S = 4096
HALF = 2048
NH = 16
NPAIR = 8
NE = 10
FE = 1408
NFC = 11
MARGIN = float(np.log(1.02))
ENGS = ("tensor", "vector", "scalar", "gpsimd", "sync")


def cvals_table():
    vals = []
    seen = {}
    for hh in range(NH):
        sl = 0.5 ** hh
        for side in range(2):
            sgn = -1.0 if side == 0 else 1.0
            for ig in range(4):
                io = ig * 512
                jcs = range(0, 4 * ig + 4) if side == 0 else range(4 * ig + 4, 32)
                for jc in jcs:
                    v = np.float32(sgn * sl * (io - jc * 128 + 192))
                    if float(v) not in seen:
                        seen[float(v)] = len(vals)
                        vals.append(float(v))
    arr = np.tile(np.asarray(vals, dtype=np.float32)[None, :], (128, 1))
    return arr, seen


def host_consts():
    c = {}
    c["ident"] = np.eye(128, dtype=BFNP)
    c["ones_row"] = np.ones((1, 1024), dtype=BFNP)
    jj = np.arange(128)[:, None]
    ii = np.arange(512)[None, :]
    relu = np.zeros((128, 4 * 512), dtype=np.float32)
    for d in range(4):
        relu[:, d * 512:(d + 1) * 512] = np.maximum(d * 128 + jj - ii, 0)
    c["relu"] = relu
    u = (np.arange(S) % 128 - 64).astype(np.float32)
    kc = np.zeros((NH, 2, 4, S), dtype=BFNP)
    for h in range(NH):
        sl = 0.5 ** h
        for side in range(2):
            sgn = 1.0 if side == 0 else -1.0
            kc[h, side, 0] = (sgn * sl * u).astype(BFNP)
            kc[h, side, 1] = 1.0 if side == 0 else 0.0
            kc[h, side, 2] = 0.0 if side == 0 else 1.0
            kc[h, side, 3] = -1.0
    c["kc"] = kc.reshape(NH * 2 * 4, S)
    iip = (np.arange(HALF) % 512 - 256).astype(np.float32)
    qc = np.zeros((NH, 3, HALF), dtype=BFNP)
    for h in range(NH):
        sl = 0.5 ** h
        qc[h, 0] = 1.0
        qc[h, 1] = (-sl * iip).astype(BFNP)
        qc[h, 2] = (+sl * iip).astype(BFNP)
    c["qc"] = qc.reshape(NH * 3, HALF)
    qm = np.zeros((128, 130), dtype=BFNP)
    qm[0:64, 64] = 1.0
    qm[64:128, 129] = 1.0
    c["qm"] = qm
    e2 = np.zeros((128, 2), dtype=BFNP)
    e2[0:64, 0] = 1.0
    e2[64:128, 1] = 1.0
    c["e2"] = e2
    return c


def host_layouts(inp):
    L = {}
    w6 = inp["L0_attn_qkv_w"].reshape(8, 2, 3, 64, D)
    L["warr"] = np.ascontiguousarray(w6.transpose(4, 0, 2, 1, 3).reshape(D, 8, 384))
    qb = inp["L0_attn_qkv_b"].reshape(8, 2, 3, 64)
    L["qkvb"] = np.ascontiguousarray(
        qb.transpose(0, 2, 1, 3).reshape(8, 3, 128).transpose(2, 0, 1).reshape(128, 24))
    L["owT"] = np.ascontiguousarray(inp["L0_attn_out_w"].T)
    L["outb"] = np.ascontiguousarray(inp["L0_attn_out_b"][None, :])
    L["gwf"] = np.ascontiguousarray(
        inp["L0_ffn_gate_w"].T.reshape(8, 128, 8).transpose(1, 0, 2).reshape(128, 64))
    swg = inp["L0_ffn_sw_gate"].reshape(D, 2, FE).transpose(1, 0, 2)
    swu = inp["L0_ffn_sw_up"].reshape(D, 2, FE).transpose(1, 0, 2)
    swd = inp["L0_ffn_sw_down"].reshape(2, FE, D)
    L["wg_all"] = np.ascontiguousarray(np.concatenate([inp["L0_ffn_w_gate"], swg], 0))
    L["wu_all"] = np.ascontiguousarray(np.concatenate([inp["L0_ffn_w_up"], swu], 0))
    L["wd_all"] = np.ascontiguousarray(np.concatenate([inp["L0_ffn_w_down"], swd], 0))
    return L


class Prog:
    def __init__(self):
        self.q = {k: [] for k in ENGS}


class Sem:
    __slots__ = ("h", "n")

    def __init__(self, h):
        self.h = h
        self.n = 0


def build(dbg=(), race=False, stage="full", precast_on=True, flash_lim=None):
    nc = bass.Bass("TRN2", target_bir_lowering=False, detect_race_conditions=race)
    xo = nc.dram_tensor("xo", [HALF, D], F32, kind="ExternalInput").ap()
    xr = nc.dram_tensor("xr", [HALF, D], F32, kind="ExternalInput").ap()
    warr = nc.dram_tensor("warr", [D, 8, 384], F32, kind="ExternalInput").ap()
    qkvb = nc.dram_tensor("qkvb", [128, 24], F32, kind="ExternalInput").ap()
    owT = nc.dram_tensor("owT", [D, D], F32, kind="ExternalInput").ap()
    outb = nc.dram_tensor("outb", [1, D], F32, kind="ExternalInput").ap()
    gwf = nc.dram_tensor("gwf", [128, 64], F32, kind="ExternalInput").ap()
    wg_all = nc.dram_tensor("wg_all", [NE, D, FE], F32, kind="ExternalInput").ap()
    wu_all = nc.dram_tensor("wu_all", [NE, D, FE], F32, kind="ExternalInput").ap()
    wd_all = nc.dram_tensor("wd_all", [NE, FE, D], F32, kind="ExternalInput").ap()
    out = nc.dram_tensor("o", [HALF, D], F32, kind="ExternalOutput").ap()

    C = host_consts()
    c_ident = nc.inline_tensor(C["ident"], "c_ident").ap()
    c_ones = nc.inline_tensor(C["ones_row"], "c_ones").ap()
    c_relu = nc.inline_tensor(C["relu"], "c_relu").ap()
    c_kc = nc.inline_tensor(C["kc"], "c_kc").ap()
    c_qc = nc.inline_tensor(C["qc"], "c_qc").ap()
    c_qm = nc.inline_tensor(C["qm"], "c_qm").ap()
    c_e2 = nc.inline_tensor(C["e2"], "c_e2").ap()
    CV_ARR, CV_MAP = cvals_table()
    c_cv = nc.inline_tensor(CV_ARR, "c_cv").ap()

    dbg_aps = {}
    for name, shape, dt in (("xnT", [256, 8 * HALF], BF16), ("comb", [8, HALF], BF16),
                            ("lg", [128, 16 * 8], F32), ("lg2", [128, 16 * 8], F32),
                            ("gwhl", [128, 128], BF16), ("hit", [128, 2048], BF16),
                            ("avp", [128, NPAIR * HALF], BF16), ("s", [128, 16 * 1024], BF16),
                            ("y", [128, 16 * 1024], F32)):
        if name in dbg:
            dbg_aps[name] = nc.dram_tensor("dbg_" + name, shape, dt,
                                           kind="ExternalOutput").ap()

    P = Prog()
    with ExitStack() as ST:
        E = ST.enter_context
        pe = Sem(E(nc.semaphore("pe")))
        ve = Sem(E(nc.semaphore("ve")))
        sc = Sem(E(nc.semaphore("sc")))
        gq = Sem(E(nc.semaphore("gq")))
        sq = Sem(E(nc.semaphore("sq")))
        tq = Sem(E(nc.semaphore("tq")))
        stq = Sem(E(nc.semaphore("stq")))
        SEM = {"tensor": pe, "vector": ve, "scalar": sc}

        def mk(eng, meth, *a, inc=None, wait=(), **kw):
            def thunk(e, _a=a, _kw=kw, _m=meth, _w=tuple(wait), _i=inc):
                for s, v in _w:
                    e.wait_ge(s.h, v)
                ins = getattr(e, _m)(*_a, **_kw)
                if _i is not None:
                    ins.then_inc(_i[0].h, _i[1])
            P.q[eng].append(thunk)

        def op(eng, meth, *a, wait=(), **kw):
            s = SEM[eng]
            wait = ((s, s.n),) + tuple(wait)
            s.n += 1
            mk(eng, meth, *a, inc=(s, 1), wait=wait, **kw)
            return s.n

        def mm(*a, wait=(), **kw):
            mk("tensor", "matmul", *a, wait=wait, **kw)

        def mmi(*a, wait=(), **kw):
            pe.n += 1
            mk("tensor", "matmul", *a, inc=(pe, 1), wait=wait, **kw)
            return pe.n

        def tpi(*a, wait=(), **kw):
            pe.n += 1
            mk("tensor", "transpose", *a, inc=(pe, 1), wait=wait, **kw)
            return pe.n

        def dummy(n, b, rhs):
            for _ in range(n):
                mm(b, sb_id[:], rhs, start=True, stop=True)

        def dma(eng, dst, src, sem, wait=()):
            sem.n += 16
            mk(eng, "dma_start", dst, src, inc=(sem, 16), wait=wait)
            return sem.n

        def dmat(dst, src, wait=()):
            tq.n += 16
            mk("sync", "dma_start_transpose", dst, src, inc=(tq, 16), wait=wait)
            return tq.n

        def wt(eng, *pairs):
            P.q[eng].append(lambda e, _p=tuple(pairs): [e.wait_ge(s.h, v) for s, v in _p])

        sb_id = E(nc.sbuf_tensor("sb_id", [128, 128], BF16))
        sb_ones = E(nc.sbuf_tensor("sb_ones", [1, 1024], BF16))
        sb_qkvb = E(nc.sbuf_tensor("sb_qkvb", [128, 24], F32))
        sb_outb = E(nc.sbuf_tensor("sb_outb", [1, 1024], BF16))
        sb_xnTA = E(nc.sbuf_tensor("sb_xnTA", [128, 8 * HALF], BF16))
        sb_c8 = E(nc.sbuf_tensor("sb_c8", [8, HALF], BF16))

        for dst, src in ((sb_id[:], c_ident), (sb_ones[:], c_ones), (sb_qkvb[:], qkvb)):
            dma("sync", dst, src, sq)
        dma("gpsimd", sb_outb[:], outb, gq)
        CONST_GQ = gq.n

        with ExitStack() as STA:
            EA = STA.enter_context
            sb_relu = EA(nc.sbuf_tensor("sb_relu", [128, 4 * 512], F32))
            sb_qm = EA(nc.sbuf_tensor("sb_qm", [128, 130], BF16))
            sb_e2 = EA(nc.sbuf_tensor("sb_e2", [128, 2], BF16))
            sb_cv = EA(nc.sbuf_tensor("sb_cv", [128, CV_ARR.shape[1]], F32))
            sb_xnTB = EA(nc.sbuf_tensor("sb_xnTB", [128, 8 * HALF], BF16))
            sb_avp = EA(nc.sbuf_tensor("sb_avp", [128, NPAIR * HALF], BF16))
            sb_gwf = EA(nc.sbuf_tensor("sb_gwf", [128, 64], F32))
            sb_gwh = EA(nc.sbuf_tensor("sb_gwh", [128, 64], BF16))
            sb_gwl = EA(nc.sbuf_tensor("sb_gwl", [128, 64], BF16))
            dma("sync", sb_gwf[:], gwf, sq)
            for dst, src_ in ((sb_relu[:], c_relu), (sb_qm[:], c_qm), (sb_e2[:], c_e2),
                              (sb_cv[:], c_cv)):
                dma("sync", dst, src_, sq)
            GWQ = dma("sync", sb_gwf[:], gwf, sq)

            with ExitStack() as STP:
                EP = STP.enter_context
                sb_xt = EP(nc.sbuf_tensor("sb_xt", [128, 2 * 1024], F32))
                sb_xn = EP(nc.sbuf_tensor("sb_xn", [128, 2 * 1024], BF16))
                sb_lo = EP(nc.sbuf_tensor("sb_lo", [128, 1024], BF16))
                sb_hiT = EP(nc.sbuf_tensor("sb_hiT", [128, 2 * 1024], BF16))
                sb_loT = EP(nc.sbuf_tensor("sb_loT", [128, 2 * 1024], BF16))
                sb_st1 = EP(nc.sbuf_tensor("sb_st1", [128, 16], F32))
                sb_fsq = EP(nc.sbuf_tensor("sb_fsq", [128, 1024], F32))
                sb_r1 = EP(nc.sbuf_tensor("sb_r1", [128, 8], F32))
                sb_r2 = EP(nc.sbuf_tensor("sb_r2", [128, 8], F32))
                sb_r3 = EP(nc.sbuf_tensor("sb_r3", [128, 8], F32))
                sb_cmb = EP(nc.sbuf_tensor("sb_cmb", [128, 8], BF16))
                ps_r = [EP(nc.psum_tensor(f"ps_r{i}", [128, 8], F32)) for i in range(2)]
                ps_tr = EP(nc.psum_tensor("ps_tr", [8, 128], BF16))
                ps_t2 = [EP(nc.psum_tensor(f"ps_t2{i}", [128, 128], BF16))
                         for i in range(2)]
                ps_dm0 = EP(nc.psum_tensor("ps_dm0", [128, 512], F32))
                if "lg" in dbg_aps:
                    sb_lg = EP(nc.sbuf_tensor("sb_lg", [128, 16 * 8], F32))
                if "lg2" in dbg_aps:
                    sb_lg2 = EP(nc.sbuf_tensor("sb_lg2", [128, 16 * 8], F32))
                    ps_r2 = EP(nc.psum_tensor("ps_r2", [128, 8], F32))

                v = op("vector", "tensor_copy", sb_gwh[:], sb_gwf[:], wait=((sq, GWQ),))
                v = op("vector", "tensor_sub", sb_fsq[:, 0:64], sb_gwf[:], sb_gwh[:],
                       wait=((ve, v),))
                VGW = op("vector", "tensor_copy", sb_gwl[:], sb_fsq[:, 0:64])

                R3 = 1.0 / D
                LAST_TP = [0, 0]
                LAST_APPLY = [0, 0]
                LAST_XUSE = [0, 0]
                LOTP_PE = 0
                T2EV = [0, 0]
                LAST_MMR = 0
                LAST_T5 = 0
                ROUTER_V = 0
                for ti in range(32):
                    own = ti < 16
                    par = ti % 2
                    t0 = (ti % 16) * 128
                    xsrc = xo if own else xr
                    xtile = sb_xt[:, par * 1024:(par + 1) * 1024]
                    xntile = sb_xn[:, par * 1024:(par + 1) * 1024]
                    w_ld = ((ve, LAST_XUSE[par]),) if ti >= 2 else ()
                    dq = dma("sync", xtile, xsrc[t0:t0 + 128, :], sq, wait=w_ld)
                    v = op("vector", "reduce_sum", sb_st1[:, 0:1], xtile, AXX,
                           wait=((sq, dq),))
                    v = op("vector", "tensor_mul", sb_fsq[:], xtile, xtile)
                    v = op("vector", "reduce_sum", sb_st1[:, 1:2], sb_fsq[:], AXX)
                    v = op("vector", "tensor_scalar", sb_st1[:, 2:3], sb_st1[:, 0:1],
                           R3, None, ALU.mult)
                    v = op("vector", "tensor_scalar", sb_st1[:, 3:4], sb_st1[:, 1:2],
                           R3, None, ALU.mult)
                    v = op("vector", "tensor_mul", sb_st1[:, 4:5], sb_st1[:, 2:3],
                           sb_st1[:, 2:3], wait=((ve, v),))
                    v = op("vector", "tensor_sub", sb_st1[:, 5:6], sb_st1[:, 3:4],
                           sb_st1[:, 4:5], wait=((ve, v),))
                    v = op("vector", "tensor_scalar", sb_st1[:, 6:7], sb_st1[:, 5:6],
                           1e-5, None, ALU.add, wait=((ve, v),))
                    cl, ce = 7 + 4 * par, 8 + 4 * par
                    cl, ce = (7, 8) if par == 0 else (9, 10)
                    s1 = op("scalar", "activation", sb_st1[:, cl:cl + 1],
                            sb_st1[:, 6:7], AF.Ln,
                            wait=((ve, v),) + (((ve, LAST_APPLY[par]),) if ti >= 2 else ()))
                    s1 = op("scalar", "activation", sb_st1[:, ce:ce + 1],
                            sb_st1[:, cl:cl + 1], AF.Exp, scale=-0.5, wait=((sc, s1),))
                    wap = ((sc, s1), (ve, v))
                    if ti >= 2:
                        wap += ((tq, LAST_TP[par]),)
                    vap = op("vector", "tensor_scalar", xntile, xtile,
                             sb_st1[:, 2:3], sb_st1[:, ce:ce + 1], ALU.subtract,
                             ALU.mult, wait=wap)
                    LAST_APPLY[par] = vap
                    LAST_XUSE[par] = vap
                    dstT = (sb_xnTA if own else sb_xnTB)[:].rearrange(
                        "p (dc t) -> p dc t", dc=8)[:, :, t0:t0 + 128]
                    LAST_TP[par] = dmat(dstT, xntile.rearrange("p (dc f) -> p dc f", dc=8),
                                        wait=((ve, vap),))
                    if own:
                        v3 = op("vector", "tensor_scalar", sb_fsq[:], xtile,
                                sb_st1[:, 2:3], sb_st1[:, ce:ce + 1], ALU.subtract,
                                ALU.mult, wait=((sc, s1), (ve, vap)))
                        wlo = ((ve, vap),)
                        if LOTP_PE:
                            wlo += ((pe, LOTP_PE),)
                        v3 = op("vector", "tensor_sub", sb_lo[:], sb_fsq[:], xntile,
                                wait=wlo)
                        LAST_XUSE[par] = v3
                        for k in range(16):
                            b = k % 2
                            if k < 8:
                                src = xntile[:, k * 128:(k + 1) * 128]
                                dstT = sb_hiT[:, par * 1024 + k * 128:
                                              par * 1024 + (k + 1) * 128]
                                wtp = ((ve, vap),) if k == 0 else ()
                            else:
                                src = sb_lo[:, (k - 8) * 128:(k - 7) * 128]
                                dstT = sb_loT[:, par * 1024 + (k - 8) * 128:
                                              par * 1024 + (k - 7) * 128]
                                wtp = ((ve, v3),) if k == 8 else ()
                            if T2EV[b]:
                                wtp += ((ve, T2EV[b]),)
                            tpk = tpi(ps_t2[b][:], src, sb_id[:], wait=wtp)
                            T2EV[b] = op("vector", "tensor_copy", dstT,
                                         ps_t2[b][:], wait=((pe, tpk),))
                        LOTP_PE = tpk
                        bank = ps_r[(ti // 2) % 2]
                        w0 = ((ve, T2EV[0]), (ve, T2EV[1]), (ve, VGW))
                        if ti >= 4:
                            w0 += ((ve, ROUTER_V),)
                        for dc in range(8):
                            xs = sb_hiT[:, par * 1024 + dc * 128:
                                        par * 1024 + (dc + 1) * 128]
                            ls = sb_loT[:, par * 1024 + dc * 128:
                                        par * 1024 + (dc + 1) * 128]
                            mm(bank[:], xs, sb_gwh[:, dc * 8:(dc + 1) * 8],
                               start=(dc == 0), stop=False, wait=w0 if dc == 0 else ())
                            mm(bank[:], xs, sb_gwl[:, dc * 8:(dc + 1) * 8],
                               start=False, stop=False)
                            if dc < 7:
                                mm(bank[:], ls, sb_gwh[:, dc * 8:(dc + 1) * 8],
                                   start=False, stop=False)
                            else:
                                MMR = mmi(bank[:], ls, sb_gwh[:, dc * 8:(dc + 1) * 8],
                                          start=False, stop=True)
                                LAST_MMR = MMR
                        if "lg2" in dbg_aps:
                            for dc in range(8):
                                xs2 = sb_hiT[:, par * 1024 + dc * 128:
                                             par * 1024 + (dc + 1) * 128]
                                ls2 = sb_loT[:, par * 1024 + dc * 128:
                                             par * 1024 + (dc + 1) * 128]
                                w2 = ((ve, LG2V),) if dc == 0 and ti >= 1 else ()
                                mm(ps_r2[:], xs2, sb_gwh[:, dc * 8:(dc + 1) * 8],
                                   start=(dc == 0), stop=False, wait=w2)
                                mm(ps_r2[:], xs2, sb_gwl[:, dc * 8:(dc + 1) * 8],
                                   start=False, stop=False)
                                MM2 = mmi(ps_r2[:], ls2,
                                          sb_gwh[:, dc * 8:(dc + 1) * 8],
                                          start=False, stop=(dc == 7))
                            LG2V = op("vector", "tensor_copy",
                                      sb_lg2[:, (ti % 16) * 8:(ti % 16) * 8 + 8],
                                      ps_r2[:], wait=((pe, MM2),))
                            dma("sync", dbg_aps["lg2"][:, (ti % 16) * 8:
                                                       (ti % 16) * 8 + 8],
                                sb_lg2[:, (ti % 16) * 8:(ti % 16) * 8 + 8], stq,
                                wait=((ve, LG2V),))
                        if ti == 8 and "hit" in dbg_aps:
                            dma("sync", dbg_aps["hit"][:, 0:1024],
                                sb_hiT[:, par * 1024:(par + 1) * 1024], stq,
                                wait=((pe, MMR),))
                            dma("sync", dbg_aps["hit"][:, 1024:2048],
                                sb_loT[:, par * 1024:(par + 1) * 1024], stq)
                        if "lg" in dbg_aps:
                            vlg = op("vector", "tensor_copy",
                                     sb_lg[:, (ti % 16) * 8:(ti % 16) * 8 + 8],
                                     bank[:], wait=((pe, MMR),))
                            dma("sync", dbg_aps["lg"][:, (ti % 16) * 8:(ti % 16) * 8 + 8],
                                sb_lg[:, (ti % 16) * 8:(ti % 16) * 8 + 8], stq,
                                wait=((ve, vlg),))
                        v4 = op("vector", "tensor_reduce", sb_st1[:, 11:12], bank[:],
                                AXX, ALU.max, negate=True, wait=((pe, MMR),))
                        s2 = op("scalar", "activation", sb_r1[:], bank[:], AF.Exp,
                                bias=sb_st1[:, 11:12], wait=((ve, v4),))
                        v4 = op("vector", "reduce_sum", sb_st1[:, 12:13], sb_r1[:],
                                AXX, wait=((sc, s2),))
                        v4 = op("vector", "reciprocal", sb_st1[:, 13:14],
                                sb_st1[:, 12:13], wait=((ve, v4),))
                        v4 = op("vector", "tensor_scalar", sb_r2[:], sb_r1[:],
                                sb_st1[:, 13:14], None, ALU.mult, wait=((ve, v4),))
                        v4 = op("vector", "reduce_max", sb_st1[:, 14:15], sb_r2[:],
                                AXX, wait=((ve, v4),))
                        v4 = op("vector", "tensor_scalar", sb_r3[:], sb_r2[:],
                                sb_st1[:, 14:15], -10.0, ALU.is_ge, ALU.mult,
                                wait=((ve, v4),))
                        v4 = op("vector", "tensor_add", sb_r3[:], sb_r3[:], sb_r2[:],
                                wait=((ve, v4),))
                        v4 = op("vector", "reduce_max", sb_st1[:, 15:16], sb_r3[:],
                                AXX, wait=((ve, v4),))
                        v4 = op("vector", "tensor_add", sb_r1[:, 0:1], sb_st1[:, 14:15],
                                sb_st1[:, 15:16], wait=((ve, v4),))
                        v4 = op("vector", "tensor_scalar", sb_r1[:, 0:1], sb_r1[:, 0:1],
                                1e-20, None, ALU.add, wait=((ve, v4),))
                        v4 = op("vector", "reciprocal", sb_r1[:, 1:2], sb_r1[:, 0:1],
                                wait=((ve, v4),))
                        v4 = op("vector", "tensor_scalar", sb_r3[:], sb_r2[:],
                                sb_st1[:, 15:16], None, ALU.is_ge, wait=((ve, v4),))
                        v4 = op("vector", "tensor_scalar", sb_r2[:], sb_r2[:],
                                sb_r1[:, 1:2], 2.5, ALU.mult, ALU.mult,
                                wait=((ve, v4),))
                        wcm = ((ve, v4),)
                        if LAST_T5:
                            wcm += ((pe, LAST_T5),)
                        v4 = op("vector", "tensor_mul", sb_cmb[:], sb_r2[:], sb_r3[:],
                                wait=wcm)
                        t5 = op("tensor", "transpose", ps_tr[:], sb_cmb[:], sb_id[:],
                                wait=((ve, v4),))
                        LAST_T5 = t5
                        ROUTER_V = op("vector", "tensor_copy", sb_c8[:, t0:t0 + 128],
                                      ps_tr[:], wait=((pe, t5),))
                        dummy(8, ps_dm0[:], sb_xnTA[:, 0:512])
                    else:
                        dummy(10, ps_dm0[:], sb_xnTA[:, 0:512])
                CMB_DONE = ROUTER_V
                if "comb" in dbg_aps:
                    dma("sync", dbg_aps["comb"], sb_c8[:], stq, wait=((ve, ROUTER_V),))
                if "gwhl" in dbg_aps:
                    dma("sync", dbg_aps["gwhl"][:, 0:64], sb_gwh[:], stq,
                        wait=((ve, ROUTER_V),))
                    dma("sync", dbg_aps["gwhl"][:, 64:128], sb_gwl[:], stq)
            P0_SQ = sq.n
            XNT_TQ = tq.n
            if "xnT" in dbg_aps:
                dma("sync", dbg_aps["xnT"][0:128, :], sb_xnTA[:], stq,
                    wait=((sq, P0_SQ), (tq, XNT_TQ)))
                dma("sync", dbg_aps["xnT"][128:256, :], sb_xnTB[:], stq)

            NPAIR_RUN = {"p0": 0, "pair1": 1, "prep1": 1}.get(stage, NPAIR)
            DO_OUT = stage in ("att", "full")
            DO_MOE = stage == "full"

            with ExitStack() as STB:
                EB = STB.enter_context
                sb_kh = [EB(nc.sbuf_tensor(f"sb_kh{h}", [68, S], BF16)) for h in range(2)]
                sb_qh = [EB(nc.sbuf_tensor(f"sb_qh{h}", [68, HALF], BF16)) for h in range(2)]
                sb_vT = EB(nc.sbuf_tensor("sb_vT", [128, S], BF16))
                sb_vh = [EB(nc.sbuf_tensor(f"sb_vh{h}", [128, 32 * 80], BF16))
                         for h in range(2)]
                sb_wp = EB(nc.sbuf_tensor("sb_wp", [128, 2 * 3072], BF16))
                sb_ex = EB(nc.sbuf_tensor("sb_ex", [128, 3 * 512], BF16))
                sb_hop = EB(nc.sbuf_tensor("sb_hop", [128, 2 * 512], BF16))
                sb_sq2 = EB(nc.sbuf_tensor("sb_sq2", [128, 2 * 512], BF16))
                sb_ms = EB(nc.sbuf_tensor("sb_ms", [1, 2 * HALF], BF16))
                sb_mh = EB(nc.sbuf_tensor("sb_mh", [1, HALF], BF16))
                sb_km = EB(nc.sbuf_tensor("sb_km", [1, 24], F32))
                sb_rr = EB(nc.sbuf_tensor("sb_rr", [65, HALF], BF16))
                sb_rb0 = EB(nc.sbuf_tensor("sb_rb0", [1, HALF], BF16))

                MEMS_V = [0, 0]
                for h in range(2):
                    MEMS_V[h] = op("vector", "memset", sb_vh[h][:].rearrange(
                        "p (c w) -> p c w", w=80)[:, :, 64:65], 1.0)

                WPQ = [0, 0]
                WPQ[0] = dma("gpsimd",
                             sb_wp[:, 0:3072].rearrange("p (dc c) -> p dc c", dc=8),
                             warr[:, 0, :].rearrange("(dc p) c -> p dc c", p=128), gq,
                             wait=((gq, CONST_GQ),))

                FLASH_END = {"pe": pe.n, "ve": ve.n, "sc": sc.n}
                for pair in range(NPAIR_RUN):
                    par = pair % 2
                    wp0 = par * 3072
                    sl_pair = [0.5 ** (2 * pair), 0.5 ** (2 * pair + 1)]

                    if pair + 1 < NPAIR:
                        WPQ[1 - par] = dma(
                            "gpsimd",
                            sb_wp[:, (1 - par) * 3072:(2 - par) * 3072].rearrange(
                                "p (dc c) -> p dc c", dc=8),
                            warr[:, pair + 1, :].rearrange("(dc p) c -> p dc c", p=128),
                            gq, wait=((pe, FLASH_END["pe"]),) if pair >= 1 else ())
                    with ExitStack() as SPP:
                        EPP = SPP.enter_context
                        ps_pj = [EPP(nc.psum_tensor(f"pj{pair}_{i}", [128, 512], F32))
                                 for i in range(2)]
                        ps_q = EPP(nc.psum_tensor(f"pq{pair}", [1, 512], F32))
                        ps_k2 = EPP(nc.psum_tensor(f"pk2{pair}", [1, 512], F32))
                        ps_dm1 = EPP(nc.psum_tensor(f"pdm{pair}", [128, 512], F32))

                        wait_prev = ((sc, FLASH_END["sc"]), (ve, FLASH_END["ve"]),
                                     (pe, FLASH_END["pe"]))
                        KQC = 0
                        for h in range(2):
                            hh = 2 * pair + h
                            KQC = dma("sync", sb_kh[h][64:68, :],
                                      c_kc[(hh * 2) * 4:(hh * 2) * 4 + 4, :], sq,
                                      wait=((pe, FLASH_END["pe"]),))
                            KQC = dma("sync", sb_qh[h][64:67, :],
                                      c_qc[hh * 3:hh * 3 + 3, :], sq)

                        def rhs_xnT(dc, j0):
                            if j0 < HALF:
                                return sb_xnTA[:, dc * HALF + j0: dc * HALF + j0 + 512]
                            j0 -= HALF
                            return sb_xnTB[:, dc * HALF + j0: dc * HALF + j0 + 512]

                        HOP_Q = [0, 0]
                        BANK_FREE_V = [0, 0]

                        def proj_tiles(kind_off, nt, evac):
                            for t in range(nt):
                                bank = ps_pj[t % 2]
                                for dc in range(8):
                                    w = ()
                                    if dc == 0:
                                        w = ((gq, WPQ[par]), (sq, KQC),
                                             (tq, XNT_TQ)) + wait_prev
                                        if BANK_FREE_V[t % 2]:
                                            w += ((ve, BANK_FREE_V[t % 2]),)
                                    lhs = sb_wp[:, wp0 + dc * 384 + kind_off:
                                                wp0 + dc * 384 + kind_off + 128]
                                    if dc < 7:
                                        mm(bank[:], lhs, rhs_xnT(dc, t * 512),
                                           start=(dc == 0), stop=False, wait=w)
                                    else:
                                        MMX = mmi(bank[:], lhs, rhs_xnT(dc, t * 512),
                                                  start=False, stop=True, wait=w)
                                BANK_FREE_V[t % 2] = evac(t, bank, MMX)

                        def evac_k(t, bank, MMX):
                            bcol = sb_qkvb[:, 3 * pair + 1: 3 * pair + 2]
                            op("vector", "tensor_scalar",
                               sb_kh[0][0:64, t * 512:(t + 1) * 512],
                               bank[0:64, :], bcol[0:64, :], None, ALU.add,
                               wait=((pe, MMX),))
                            wB = ((sq, HOP_Q[t % 2]),) if HOP_Q[t % 2] else ()
                            vB = op("vector", "tensor_scalar",
                                    sb_hop[64:128, (t % 2) * 512:(t % 2 + 1) * 512],
                                    bank[64:128, :], bcol[64:128, :], None, ALU.add,
                                    wait=wB)
                            HOP_Q[t % 2] = dma(
                                "sync", sb_kh[1][0:64, t * 512:(t + 1) * 512],
                                sb_hop[64:128, (t % 2) * 512:(t % 2 + 1) * 512],
                                sq, wait=((ve, vB),))
                            return vB

                        proj_tiles(128, 8, evac_k)
                        HOPK_LAST = sq.n

                        def evac_v(t, bank, MMX):
                            bcol = sb_qkvb[:, 3 * pair + 2: 3 * pair + 3]
                            return op("vector", "tensor_scalar",
                                      sb_vT[:, t * 512:(t + 1) * 512], bank[:],
                                      bcol, None, ALU.add, wait=((pe, MMX),))

                        proj_tiles(256, 8, evac_v)
                        VEV = ve.n
                        VHQ = 0
                        for h in range(2):
                            wv = ((ve, VEV), (ve, MEMS_V[h]))
                            if pair >= 1:
                                wv += ((pe, FLASH_END["pe"]),)
                            VHQ = dmat(sb_vh[h][:].rearrange(
                                "p (c w) -> p c w", w=80)[:, :, 0:64],
                                sb_vT[h * 64:(h + 1) * 64, :].rearrange(
                                    "p (c f) -> p c f", c=32), wait=wv)

                        def evac_q(t, bank, MMX):
                            bcol = sb_qkvb[:, 3 * pair: 3 * pair + 1]
                            op("vector", "tensor_scalar",
                               sb_qh[0][0:64, t * 512:(t + 1) * 512],
                               bank[0:64, :], bcol[0:64, :], 0.125, ALU.add,
                               ALU.mult, wait=((pe, MMX),))
                            wB = ((sq, HOP_Q[t % 2]),)
                            vB = op("vector", "tensor_scalar",
                                    sb_hop[64:128, (t % 2) * 512:(t % 2 + 1) * 512],
                                    bank[64:128, :], bcol[64:128, :], 0.125, ALU.add,
                                    ALU.mult, wait=wB)
                            HOP_Q[t % 2] = dma(
                                "sync", sb_qh[1][0:64, t * 512:(t + 1) * 512],
                                sb_hop[64:128, (t % 2) * 512:(t % 2 + 1) * 512],
                                sq, wait=((ve, vB),))
                            return vB

                        proj_tiles(0, 4, evac_q)
                        HOPQ_LAST = sq.n
                        QKEV = ve.n

                        onescol = sb_qm[0:64, 64:65]
                        SQMM = [0, 0]
                        KMR = 0
                        for h in range(2):
                            src = sb_kh[h]
                            for t in range(8):
                                slot = t % 2
                                w = ((ve, QKEV),) if h == 0 else ((sq, HOPK_LAST),)
                                if SQMM[slot]:
                                    w += ((pe, SQMM[slot]),)
                                sS = op("scalar", "activation",
                                        sb_sq2[0:64, slot * 512:(slot + 1) * 512],
                                        src[0:64, t * 512:(t + 1) * 512],
                                        AF.Square, wait=w)
                                wm = ((sc, sS),)
                                if KMR:
                                    wm += ((ve, KMR),)
                                SQMM[slot] = mmi(ps_k2[0:1, :], onescol,
                                                 sb_sq2[0:64, slot * 512:(slot + 1) * 512],
                                                 start=True, stop=True, wait=wm)
                                KMR = op("vector", "reduce_max",
                                         sb_km[0:1, h * 8 + t: h * 8 + t + 1],
                                         ps_k2[0:1, :], AXX, wait=((pe, SQMM[slot]),))
                                dummy(2, ps_dm1[:], sb_xnTA[:, 0:512])
                        v = op("vector", "reduce_max", sb_km[0:1, 16:17],
                               sb_km[0:1, 0:8], AXX, wait=((ve, KMR),))
                        v = op("vector", "reduce_max", sb_km[0:1, 17:18],
                               sb_km[0:1, 8:16], AXX, wait=((ve, v),))
                        skm = op("scalar", "activation", sb_km[0:1, 18:20],
                                 sb_km[0:1, 16:18], AF.Ln, wait=((ve, v),))
                        KMBV = op("vector", "tensor_scalar", sb_km[0:1, 20:22],
                                  sb_km[0:1, 18:20], 0.5, MARGIN, ALU.mult, ALU.add,
                                  wait=((sc, skm),))
                        MH_Q = 0
                        for h in range(2):
                            LNQ = 0
                            for t in range(4):
                                slot = t % 2
                                w = ((ve, QKEV),) if h == 0 else ((sq, HOPQ_LAST),)
                                if SQMM[slot]:
                                    w += ((pe, SQMM[slot]),)
                                sS = op("scalar", "activation",
                                        sb_sq2[0:64, slot * 512:(slot + 1) * 512],
                                        sb_qh[h][0:64, t * 512:(t + 1) * 512],
                                        AF.Square, wait=w)
                                wm = ((sc, sS),)
                                if LNQ:
                                    wm += ((sc, LNQ),)
                                SQMM[slot] = mmi(ps_q[0:1, :], onescol,
                                                 sb_sq2[0:64, slot * 512:(slot + 1) * 512],
                                                 start=True, stop=True, wait=wm)
                                LNQ = op("scalar", "activation",
                                         sb_ms[0:1, h * HALF + t * 512:
                                               h * HALF + (t + 1) * 512],
                                         ps_q[0:1, :], AF.Ln, wait=((pe, SQMM[slot]),))
                                dummy(2, ps_dm1[:], sb_xnTA[:, 0:512])
                            w = ((sc, LNQ), (ve, KMBV))
                            if h == 1:
                                w += ((sq, MH_Q),)
                            sE = op("scalar", "activation", sb_mh[0:1, :],
                                    sb_ms[0:1, h * HALF:(h + 1) * HALF], AF.Exp,
                                    scale=0.5, bias=sb_km[0:1, 20 + h:21 + h], wait=w)
                            MH_Q = dma("sync", sb_qh[h][67:68, :], sb_mh[0:1, :],
                                       sq, wait=((sc, sE),))
                            dummy(4, ps_dm1[:], sb_xnTA[:, 0:512])
                        PREP = {"ve": ve.n, "sc": sc.n, "sq": sq.n, "pe": pe.n}

                    with ExitStack() as SPF:
                        EPF = SPF.enter_context
                        ps_sc = [EPF(nc.psum_tensor(f"sc{pair}_{i}", [128, 512], F32))
                                 for i in range(3)]
                        ps_av = [EPF(nc.psum_tensor(f"av{pair}_{i}", [65, 512], F32))
                                 for i in range(4)]
                        ps_bc = EPF(nc.psum_tensor(f"bc{pair}", [128, 512], F32))

                        AV_FREE = [FLASH_END["ve"]] * 4
                        BC_FREE = FLASH_END["ve"]
                        RB_Q = 0
                        LAST_MB = 0
                        scnt = 0
                        EXP_S = {}
                        AV_PE = {}
                        NH_RUN, NS_RUN = flash_lim if flash_lim else (2, 2)
                        if stage == "prep1":
                            NH_RUN = 0
                        for h in range(NH_RUN):
                            sl = sl_pair[h]
                            KEPT = [[], []]
                            for side_ in range(2):
                                for ig_ in range(4):
                                    io_ = ig_ * 512
                                    js = (range(0, 4 * ig_ + 4) if side_ == 0
                                          else range(4 * ig_ + 4, 32))
                                    KEPT[side_].append(
                                        [jc_ for jc_ in js
                                         if sl * max(0, io_ - (jc_ * 128 + 127),
                                                     jc_ * 128 - (io_ + 511)) < 110.0])
                            last_of = []
                            for ig_ in range(4):
                                if NS_RUN == 2 and KEPT[1][ig_]:
                                    last_of.append((1, KEPT[1][ig_][-1]))
                                else:
                                    last_of.append((0, KEPT[0][ig_][-1]))
                            SIDE_Q = 0
                            SKEW = 3
                            pend = []

                            def emit_av(sct, igx, jcx, first, last):
                                wav = ((sc, EXP_S[sct]), (tq, VHQ))
                                if first:
                                    wav += ((ve, AV_FREE[igx]),)
                                AV_PE[sct] = mmi(
                                    ps_av[igx][:],
                                    sb_vh[h][:, jcx * 80: jcx * 80 + 65],
                                    sb_ex[:, (sct % 3) * 512:(sct % 3 + 1) * 512],
                                    start=first, stop=last, wait=wav)
                            for side in range(NS_RUN):
                                if side == 1:
                                    hh = 2 * pair + h
                                    SIDE_Q = dma("sync", sb_kh[h][64:68, :],
                                                 c_kc[(hh * 2 + 1) * 4:
                                                      (hh * 2 + 1) * 4 + 4, :], sq,
                                                 wait=((pe, pe.n),))
                                for ig in range(4):
                                    io = ig * 512
                                    jcs = KEPT[side][ig]
                                    first_side1 = jcs[0] if jcs else -1
                                    for jc in jcs:
                                        jo = jc * 128
                                        sgn = -1.0 if side == 0 else 1.0
                                        Cb = sgn * sl * (io - jo + 192)
                                        bank = ps_sc[scnt % 3]
                                        w = ((sq, max(PREP["sq"], SIDE_Q)),
                                             (ve, PREP["ve"]), (sc, PREP["sc"]))
                                        if scnt >= 3:
                                            w = ((sc, EXP_S[scnt - 3]),)
                                            if side == 1 and jc == first_side1:
                                                w += ((sq, SIDE_Q),)
                                        MS = mmi(bank[:],
                                                 sb_kh[h][0:68, jo:jo + 128],
                                                 sb_qh[h][0:68, io:io + 512],
                                                 start=True, stop=True, wait=w)
                                        if len(pend) >= SKEW:
                                            emit_av(*pend.pop(0))
                                        diag = (side == 0 and 4 * ig <= jc <= 4 * ig + 3)
                                        wx = ((pe, MS),)
                                        if diag:
                                            dv = jc - 4 * ig
                                            vd = op("vector", "scalar_tensor_tensor",
                                                    bank[:],
                                                    sb_relu[:, dv * 512:(dv + 1) * 512],
                                                    -2.0 * sl, bank[:], ALU.mult,
                                                    ALU.add, wait=((pe, MS),))
                                            wx = ((ve, vd),)
                                        if scnt >= 3:
                                            wx += ((pe, AV_PE[scnt - 3]),)
                                        cvc = CV_MAP[float(np.float32(Cb))]
                                        EXP_S[scnt] = op(
                                            "scalar", "activation",
                                            sb_ex[:, (scnt % 3) * 512:(scnt % 3 + 1) * 512],
                                            bank[:], AF.Exp,
                                            bias=sb_cv[:, cvc:cvc + 1], wait=wx)
                                        first = (side == 0 and jc == KEPT[0][ig][0])
                                        last = ((side, jc) == last_of[ig])
                                        pend.append((scnt, ig, jc, first, last))
                                        scnt += 1
                            while pend:
                                emit_av(*pend.pop(0))
                            HOPA = [0, 0]
                            RECV = 0
                            LAST_AV = AV_PE[scnt - 1]
                            for ig in range(4):
                                cols = slice(pair * HALF + ig * 512,
                                             pair * HALF + (ig + 1) * 512)
                                if h == 0:
                                    vE = op("vector", "tensor_copy",
                                            sb_avp[0:64, cols], ps_av[ig][0:64, :],
                                            wait=((pe, LAST_AV),))
                                else:
                                    w = ((pe, LAST_AV),)
                                    if ig >= 2:
                                        w += ((sq, HOPA[ig % 2]),)
                                    vE = op("vector", "tensor_copy",
                                            sb_hop[0:64, (ig % 2) * 512:(ig % 2 + 1) * 512],
                                            ps_av[ig][0:64, :], wait=w)
                                    HOPA[ig % 2] = dma("sync", sb_avp[64:128, cols],
                                                       sb_hop[0:64, (ig % 2) * 512:
                                                              (ig % 2 + 1) * 512],
                                                       sq, wait=((ve, vE),))
                                wrc = ((pe, LAST_AV),)
                                if h == 1:
                                    wrc += ((sq, RB_Q),)
                                RECV = op("vector", "reciprocal",
                                          sb_rr[64:65, ig * 512:(ig + 1) * 512],
                                          ps_av[ig][64:65, :], wait=wrc)
                                AV_FREE[ig] = max(vE, RECV)
                                dummy(2, ps_sc[0][:], sb_xnTA[:, 0:512])
                            wrb = ((ve, RECV),)
                            if h == 1:
                                wrb += ((pe, LAST_MB),)
                            RB_Q = dma("sync", sb_rb0[:], sb_rr[64:65, :], sq,
                                       wait=wrb)
                            for ig in range(4):
                                wbc = ((sq, RB_Q), (ve, BC_FREE))
                                tp = (0, 64 * h)
                                MB = mmi(ps_bc[64 * h:64 * h + 64, :],
                                         sb_ones[0:1, 0:64],
                                         sb_rb0[0:1, ig * 512:(ig + 1) * 512],
                                         start=True, stop=True, tile_position=tp,
                                         wait=wbc)
                                LAST_MB = MB
                                cols = slice(pair * HALF + ig * 512,
                                             pair * HALF + (ig + 1) * 512)
                                w = ((pe, MB),)
                                if h == 1:
                                    w += ((sq, HOPA[ig % 2]),)
                                BC_FREE = op("vector", "tensor_tensor",
                                             sb_avp[64 * h:64 * h + 64, cols],
                                             sb_avp[64 * h:64 * h + 64, cols],
                                             ps_bc[64 * h:64 * h + 64, :], ALU.mult,
                                             wait=w)
                                dummy(2, ps_sc[0][:], sb_xnTA[:, 0:512])
                        FLASH_END = {"pe": pe.n, "ve": ve.n, "sc": sc.n}
                AVP_DONE = {"ve": ve.n, "sq": sq.n, "pe": pe.n, "sc": sc.n}

            if "avp" in dbg_aps:
                dma("sync", dbg_aps["avp"], sb_avp[:], stq,
                    wait=((ve, AVP_DONE["ve"]), (sq, AVP_DONE["sq"])))

            sb_s = ST.enter_context(
                nc.sbuf_tensor("sb_s", [128, 16 * 1024], BF16, side="right"))
            with ExitStack() as STO:
                EO = STO.enter_context
                sb_ow = EO(nc.sbuf_tensor("sb_ow", [128, 8 * 1024], BF16))
                ps_o = [EO(nc.psum_tensor(f"ps_o{i}", [128, 512], F32))
                        for i in range(2)]
                OWQ = dma("gpsimd", sb_ow[:].rearrange("p (dc n) -> p dc n", dc=8),
                          owT.rearrange("(dc p) n -> p dc n", p=128), gq,
                          wait=((pe, AVP_DONE["pe"]), (sc, AVP_DONE["sc"]),
                                (ve, AVP_DONE["ve"]), (sq, AVP_DONE["sq"])))
                for tt in range(16 if DO_OUT else 0):
                    for nn in range(2):
                        bank = ps_o[(tt * 2 + nn) % 2]
                        w0 = ((gq, OWQ), (ve, AVP_DONE["ve"]), (sq, AVP_DONE["sq"]))
                        if tt * 2 + nn >= 2:
                            w0 = ((ve, SFREE[(tt * 2 + nn) % 2]),)
                        for p in range(8):
                            mm(bank[:], sb_avp[:, p * HALF + tt * 128:
                                               p * HALF + tt * 128 + 128],
                               sb_ow[:, p * 1024 + nn * 512: p * 1024 + nn * 512 + 512],
                               start=(p == 0), stop=False, wait=w0 if p == 0 else ())
                        MO = mmi(bank[:], sb_ones[0:1, 0:128],
                                 sb_outb[0:1, nn * 512:(nn + 1) * 512],
                                 start=False, stop=True)
                        SFREE = SFREE if tt * 2 + nn >= 2 else [0, 0]
                        SFREE[(tt * 2 + nn) % 2] = op(
                            "vector", "tensor_copy",
                            sb_s[:, tt * 1024 + nn * 512: tt * 1024 + nn * 512 + 512],
                            bank[:], wait=((pe, MO),))
                S_DONE = ve.n
            if "s" in dbg_aps:
                dma("sync", dbg_aps["s"], sb_s[:], stq, wait=((ve, S_DONE),))

        with ExitStack() as STM:
            EM = STM.enter_context
            sb_w = [EM(nc.sbuf_tensor(f"sb_w{i}", [128, 11264], BF16)) for i in range(2)]
            sb_gr = EM(nc.sbuf_tensor("sb_gr", [128, 2 * 11 * 512], BF16))
            sb_h = EM(nc.sbuf_tensor("sb_h", [128, 2 * 11 * 512], BF16))
            sb_cb = EM(nc.sbuf_tensor("sb_cb", [128, 1024], BF16))
            sb_cbr = EM(nc.sbuf_tensor("sb_cbr", [1, 1024], BF16))
            sb_y = EM(nc.sbuf_tensor("sb_y", [128, 8 * 1024], F32))
            sb_xs = EM(nc.sbuf_tensor("sb_xs", [128, 1024], F32))
            ps_g = [EM(nc.psum_tensor(f"ps_g{i}", [128, 512], F32)) for i in range(2)]
            ps_d = [EM(nc.psum_tensor(f"ps_d{i}", [128, 512], F32)) for i in range(2)]
            ps_cb = EM(nc.psum_tensor("ps_cb", [128, 512], F32))

            STQ_TILES = 0
            OP_PE = pe.n
            WFREE = [OP_PE, OP_PE]
            LQ = {"wg": 0, "wu": 0, "wd": 0, "cb": S_DONE}
            PREV_HEV = [S_DONE, S_DONE]
            PREV_YV = [S_DONE, S_DONE]
            LAST_MCB = OP_PE
            for half in range(2 if DO_MOE else 0):
                hoff = half * 1024
                for e in range(NE):
                    gbuf, ubuf = e % 2, 1 - e % 2
                    wgq = dma("gpsimd", sb_w[gbuf][:].rearrange(
                        "p (dc f) -> p dc f", dc=8),
                        wg_all[e].rearrange("(dc p) f -> p dc f", p=128), gq,
                        wait=((pe, WFREE[gbuf]),))
                    if e < 8:
                        cbq = dma("sync", sb_cbr[:],
                                  sb_c8[e:e + 1, hoff:hoff + 1024], sq,
                                  wait=((pe, LAST_MCB),))
                        for half128 in range(2):
                            for nck in range(2):
                                MCB = mmi(ps_cb[64 * half128:64 * half128 + 64, :],
                                          sb_ones[0:1, 0:64],
                                          sb_cbr[0:1, nck * 512:(nck + 1) * 512],
                                          start=True, stop=True,
                                          tile_position=(0, 64 * half128),
                                          wait=((sq, cbq), (ve, LQ["cb"])))
                                LAST_MCB = MCB
                                LQ["cb"] = op("vector", "tensor_copy",
                                              sb_cb[64 * half128:64 * half128 + 64,
                                                    nck * 512:(nck + 1) * 512],
                                              ps_cb[64 * half128:64 * half128 + 64, :],
                                              wait=((pe, MCB),))
                    GFREE = [ve.n, sc.n]
                    SIL = {}
                    for tc in range(2):
                        for fc in range(NFC):
                            k = tc * NFC + fc
                            bank = ps_g[k % 2]
                            for dc in range(8):
                                w = ()
                                if dc == 0:
                                    w = ((gq, wgq),)
                                    if k >= 2:
                                        w += ((sc, SIL[k - 2]),)
                                    else:
                                        w += ((ve, PREV_HEV[k]),)
                                lhs = sb_w[gbuf][:, dc * FE + fc * 128:
                                                 dc * FE + fc * 128 + 128]
                                rhs_ = sb_xnTA[:, dc * HALF + hoff + tc * 512:
                                               dc * HALF + hoff + tc * 512 + 512]
                                if dc < 7:
                                    mm(bank[:], lhs, rhs_, start=(dc == 0),
                                       stop=False, wait=w)
                                else:
                                    MMG = mmi(bank[:], lhs, rhs_, start=False,
                                              stop=True, wait=w)
                            SIL[k] = op("scalar", "activation",
                                        sb_gr[:, k * 512:(k + 1) * 512], bank[:],
                                        AF.Silu, wait=((pe, MMG),))
                    GATE_PE = pe.n
                    wuq = dma("gpsimd", sb_w[ubuf][:].rearrange(
                        "p (dc f) -> p dc f", dc=8),
                        wu_all[e].rearrange("(dc p) f -> p dc f", p=128), gq,
                        wait=((pe, WFREE[ubuf]),))
                    HEV = {}
                    for tc in range(2):
                        for fc in range(NFC):
                            k = tc * NFC + fc
                            bank = ps_g[k % 2]
                            for dc in range(8):
                                w = ()
                                if dc == 0:
                                    w = ((gq, wuq),)
                                    if k >= 2:
                                        w += ((ve, HEV[k - 2]),)
                                    else:
                                        w += ((sc, SIL[20 + k]),)
                                lhs = sb_w[ubuf][:, dc * FE + fc * 128:
                                                 dc * FE + fc * 128 + 128]
                                rhs_ = sb_xnTA[:, dc * HALF + hoff + tc * 512:
                                               dc * HALF + hoff + tc * 512 + 512]
                                if dc < 7:
                                    mm(bank[:], lhs, rhs_, start=(dc == 0),
                                       stop=False, wait=w)
                                else:
                                    MMU = mmi(bank[:], lhs, rhs_, start=False,
                                              stop=True, wait=w)
                            v = op("vector", "tensor_tensor",
                                   sb_h[:, k * 512:(k + 1) * 512],
                                   sb_gr[:, k * 512:(k + 1) * 512], bank[:],
                                   ALU.mult, wait=((pe, MMU), (sc, SIL[k])))
                            if e < 8:
                                v = op("vector", "tensor_tensor",
                                       sb_h[:, k * 512:(k + 1) * 512],
                                       sb_h[:, k * 512:(k + 1) * 512],
                                       sb_cb[:, tc * 512:(tc + 1) * 512], ALU.mult,
                                       wait=((ve, LQ["cb"]),))
                            HEV[k] = v
                    UP_PE = pe.n
                    wdq = dma("gpsimd", sb_w[gbuf][:].rearrange(
                        "p (fc d) -> p fc d", fc=NFC),
                        wd_all[e].rearrange("(fc p) d -> p fc d", p=128), gq,
                        wait=((pe, GATE_PE),))
                    YV = {}
                    dn = 0
                    for tc in range(2):
                        for ts in range(4):
                            for dsl in range(2):
                                bank = ps_d[dn % 2]
                                for fc in range(NFC):
                                    w = ()
                                    if fc == 0:
                                        w = ((gq, wdq), (ve, HEV[tc * NFC + NFC - 1]))
                                        if dn >= 2:
                                            w += ((ve, YV[dn - 2]),)
                                        else:
                                            w += ((ve, PREV_YV[dn]),)
                                    lhs = sb_h[:, (tc * NFC + fc) * 512 + ts * 128:
                                               (tc * NFC + fc) * 512 + ts * 128 + 128]
                                    rhs_ = sb_w[gbuf][:, fc * 1024 + dsl * 512:
                                                      fc * 1024 + dsl * 512 + 512]
                                    if fc < NFC - 1:
                                        mm(bank[:], lhs, rhs_, start=(fc == 0),
                                           stop=False, wait=w)
                                    else:
                                        MMD = mmi(bank[:], lhs, rhs_, start=False,
                                                  stop=True, wait=w)
                                ycols = slice((tc * 4 + ts) * 1024 + dsl * 512,
                                              (tc * 4 + ts) * 1024 + dsl * 512 + 512)
                                if e == 0:
                                    YV[dn] = op("vector", "tensor_copy",
                                                sb_y[:, ycols], bank[:],
                                                wait=((pe, MMD),))
                                else:
                                    YV[dn] = op("vector", "tensor_tensor",
                                                sb_y[:, ycols], sb_y[:, ycols],
                                                bank[:], ALU.add, wait=((pe, MMD),))
                                dn += 1
                    WFREE[gbuf] = pe.n
                    WFREE[ubuf] = UP_PE
                    PREV_HEV = [HEV[20], HEV[21]]
                    PREV_YV = [YV[14], YV[15]]
                    Y_DONE = ve.n
                if "y" in dbg_aps:
                    ytap = dma("sync", dbg_aps["y"][:, half * 8192:(half + 1) * 8192],
                               sb_y[:], stq, wait=((ve, Y_DONE),))
                    wt("scalar", (stq, ytap))
                XS_FREE = 0
                for tt in range(8):
                    ycols = slice(tt * 1024, (tt + 1) * 1024)
                    sg = op("scalar", "activation", sb_gr[:, 0:1024],
                            sb_y[:, ycols], AF.Silu, wait=((ve, Y_DONE),))
                    xq = dma("sync", sb_xs[:], xo[hoff + tt * 128: hoff + tt * 128 + 128, :],
                             sq, wait=((ve, XS_FREE),))
                    v = op("vector", "tensor_tensor", sb_y[:, ycols],
                           sb_s[:, (half * 8 + tt) * 1024:(half * 8 + tt + 1) * 1024],
                           sb_gr[:, 0:1024], ALU.mult, wait=((sc, sg),))
                    XS_FREE = op("vector", "tensor_add", sb_y[:, ycols],
                                 sb_y[:, ycols], sb_xs[:], wait=((ve, v), (sq, xq)))
                    STQ_TILES = dma("sync",
                                    out[hoff + tt * 128: hoff + tt * 128 + 128, :],
                                    sb_y[:, ycols], stq, wait=((ve, XS_FREE),))
                    Y_DONE = max(Y_DONE, XS_FREE)
        wt("sync", (stq, stq.n))

        def replay(q):
            def go(e):
                with nc.allow_low_precision(reason="bf16 matmul operands by design"):
                    for f in q:
                        f(e)
            return go

        with nc.Block() as block:
            block.gpsimd(replay(P.q["gpsimd"]))
            block.scalar(replay(P.q["scalar"]))
            block.tensor(replay(P.q["tensor"]))
            block.vector(replay(P.q["vector"]))
            block.sync(replay(P.q["sync"]))
    return nc


_NC_CACHE = {}


def get_nc(dbg=()):
    key = tuple(sorted(dbg))
    if key not in _NC_CACHE:
        _NC_CACHE[key] = build(dbg=key)
    return _NC_CACHE[key]


def make_inmaps(inputs):
    L = host_layouts(inputs)
    x = np.asarray(inputs["L0_attn_x"])
    shared = {k: L[k] for k in ("warr", "qkvb", "owT", "outb", "gwf",
                                "wg_all", "wu_all", "wd_all")}
    in_maps = []
    for core in range(8):
        b, half = core // 2, core % 2
        if half == 0:
            xo_ = x[b, 0:HALF]
            xr_ = x[b, HALF:S]
        else:
            xo_ = x[b, S - 1:HALF - 1:-1]
            xr_ = x[b, HALF - 1::-1]
        m = dict(shared)
        m["xo"] = np.ascontiguousarray(xo_)
        m["xr"] = np.ascontiguousarray(xr_)
        in_maps.append(m)
    return in_maps


def kernel(**inputs):
    nc = get_nc()
    in_maps = make_inmaps(inputs)
    res = run_bass_kernel_spmd(nc, in_maps, core_ids=list(range(8)))
    out = np.empty((4, S, D), dtype=np.float32)
    for core in range(8):
        b, half = core // 2, core % 2
        o = res.results[core]["o"]
        if half == 0:
            out[b, 0:HALF] = o
        else:
            out[b, HALF:S] = o[::-1]
    return out


# version 11 (final)
# speedup vs baseline: 1.2963x; 1.2963x over previous
import numpy as np
import ml_dtypes
from contextlib import ExitStack

import concourse.bass as bass
import concourse.mybir as mybir
from concourse.bass_utils import run_bass_kernel_spmd

F32 = mybir.dt.float32
BF16 = mybir.dt.bfloat16
AF = mybir.ActivationFunctionType
ALU = mybir.AluOpType
AXX = mybir.AxisListType.X
BFNP = ml_dtypes.bfloat16

D = 1024
S = 4096
HALF = 2048
NH = 16
NPAIR = 8
NE = 10
FE = 1408
NFC = 11
MARGIN = float(np.log(1.02))
ENGS = ("tensor", "vector", "scalar", "gpsimd", "sync")


def cvals_table():
    vals = []
    seen = {}
    for hh in range(NH):
        sl = 0.5 ** hh
        for side in range(2):
            sgn = -1.0 if side == 0 else 1.0
            for ig in range(4):
                io = ig * 512
                jcs = range(0, 4 * ig + 4) if side == 0 else range(4 * ig + 4, 32)
                for jc in jcs:
                    v = np.float32(sgn * sl * (io - jc * 128 + 192))
                    if float(v) not in seen:
                        seen[float(v)] = len(vals)
                        vals.append(float(v))
    arr = np.tile(np.asarray(vals, dtype=np.float32)[None, :], (128, 1))
    return arr, seen


def host_consts():
    c = {}
    c["ident"] = np.eye(128, dtype=BFNP)
    c["ones_row"] = np.ones((1, 1024), dtype=BFNP)
    jj = np.arange(128)[:, None]
    ii = np.arange(512)[None, :]
    relu = np.zeros((128, 4 * 512), dtype=np.float32)
    for d in range(4):
        relu[:, d * 512:(d + 1) * 512] = np.maximum(d * 128 + jj - ii, 0)
    c["relu"] = relu
    u = (np.arange(S) % 128 - 64).astype(np.float32)
    kc = np.zeros((NH, 2, 4, S), dtype=BFNP)
    for h in range(NH):
        sl = 0.5 ** h
        for side in range(2):
            sgn = 1.0 if side == 0 else -1.0
            kc[h, side, 0] = (sgn * sl * u).astype(BFNP)
            kc[h, side, 1] = 1.0 if side == 0 else 0.0
            kc[h, side, 2] = 0.0 if side == 0 else 1.0
            kc[h, side, 3] = -1.0
    c["kc"] = kc.reshape(NH * 2 * 4, S)
    iip = (np.arange(HALF) % 512 - 256).astype(np.float32)
    qc = np.zeros((NH, 3, HALF), dtype=BFNP)
    for h in range(NH):
        sl = 0.5 ** h
        qc[h, 0] = 1.0
        qc[h, 1] = (-sl * iip).astype(BFNP)
        qc[h, 2] = (+sl * iip).astype(BFNP)
    c["qc"] = qc.reshape(NH * 3, HALF)
    qm = np.zeros((128, 130), dtype=BFNP)
    qm[0:64, 64] = 1.0
    qm[64:128, 129] = 1.0
    c["qm"] = qm
    e2 = np.zeros((128, 2), dtype=BFNP)
    e2[0:64, 0] = 1.0
    e2[64:128, 1] = 1.0
    c["e2"] = e2
    return c


def host_layouts(inp):
    L = {}
    w6 = inp["L0_attn_qkv_w"].reshape(8, 2, 3, 64, D)
    L["warr"] = np.ascontiguousarray(w6.transpose(4, 0, 2, 1, 3).reshape(D, 8, 384))
    qb = inp["L0_attn_qkv_b"].reshape(8, 2, 3, 64)
    L["qkvb"] = np.ascontiguousarray(
        qb.transpose(0, 2, 1, 3).reshape(8, 3, 128).transpose(2, 0, 1).reshape(128, 24))
    L["owT"] = np.ascontiguousarray(inp["L0_attn_out_w"].T)
    L["outb"] = np.ascontiguousarray(inp["L0_attn_out_b"][None, :])
    L["gwf"] = np.ascontiguousarray(
        inp["L0_ffn_gate_w"].T.reshape(8, 128, 8).transpose(1, 0, 2).reshape(128, 64))
    swg = inp["L0_ffn_sw_gate"].reshape(D, 2, FE).transpose(1, 0, 2)
    swu = inp["L0_ffn_sw_up"].reshape(D, 2, FE).transpose(1, 0, 2)
    swd = inp["L0_ffn_sw_down"].reshape(2, FE, D)
    L["wg_all"] = np.ascontiguousarray(np.concatenate([inp["L0_ffn_w_gate"], swg], 0))
    L["wu_all"] = np.ascontiguousarray(np.concatenate([inp["L0_ffn_w_up"], swu], 0))
    L["wd_all"] = np.ascontiguousarray(np.concatenate([inp["L0_ffn_w_down"], swd], 0))
    return L


class Prog:
    def __init__(self):
        self.q = {k: [] for k in ENGS}


class Sem:
    __slots__ = ("h", "n")

    def __init__(self, h):
        self.h = h
        self.n = 0


def build(dbg=(), race=False, stage="full", precast_on=True, flash_lim=None):
    nc = bass.Bass("TRN2", target_bir_lowering=False, detect_race_conditions=race)
    xo = nc.dram_tensor("xo", [HALF, D], F32, kind="ExternalInput").ap()
    xr = nc.dram_tensor("xr", [HALF, D], F32, kind="ExternalInput").ap()
    warr = nc.dram_tensor("warr", [D, 8, 384], F32, kind="ExternalInput").ap()
    qkvb = nc.dram_tensor("qkvb", [128, 24], F32, kind="ExternalInput").ap()
    owT = nc.dram_tensor("owT", [D, D], F32, kind="ExternalInput").ap()
    outb = nc.dram_tensor("outb", [1, D], F32, kind="ExternalInput").ap()
    gwf = nc.dram_tensor("gwf", [128, 64], F32, kind="ExternalInput").ap()
    wg_all = nc.dram_tensor("wg_all", [NE, D, FE], F32, kind="ExternalInput").ap()
    wu_all = nc.dram_tensor("wu_all", [NE, D, FE], F32, kind="ExternalInput").ap()
    wd_all = nc.dram_tensor("wd_all", [NE, FE, D], F32, kind="ExternalInput").ap()
    out = nc.dram_tensor("o", [HALF, D], F32, kind="ExternalOutput").ap()

    C = host_consts()
    c_ident = nc.inline_tensor(C["ident"], "c_ident").ap()
    c_ones = nc.inline_tensor(C["ones_row"], "c_ones").ap()
    c_relu = nc.inline_tensor(C["relu"], "c_relu").ap()
    c_kc = nc.inline_tensor(C["kc"], "c_kc").ap()
    c_qc = nc.inline_tensor(C["qc"], "c_qc").ap()
    c_qm = nc.inline_tensor(C["qm"], "c_qm").ap()
    c_e2 = nc.inline_tensor(C["e2"], "c_e2").ap()
    CV_ARR, CV_MAP = cvals_table()
    c_cv = nc.inline_tensor(CV_ARR, "c_cv").ap()

    dbg_aps = {}
    for name, shape, dt in (("xnT", [256, 8 * HALF], BF16), ("comb", [8, HALF], BF16),
                            ("lg", [128, 16 * 8], F32), ("lg2", [128, 16 * 8], F32),
                            ("gwhl", [128, 128], BF16), ("hit", [128, 2048], BF16),
                            ("avp", [128, NPAIR * HALF], BF16), ("s", [128, 16 * 1024], BF16),
                            ("y", [128, 16 * 1024], F32)):
        if name in dbg:
            dbg_aps[name] = nc.dram_tensor("dbg_" + name, shape, dt,
                                           kind="ExternalOutput").ap()

    P = Prog()
    with ExitStack() as ST:
        E = ST.enter_context
        pe = Sem(E(nc.semaphore("pe")))
        ve = Sem(E(nc.semaphore("ve")))
        sc = Sem(E(nc.semaphore("sc")))
        gq = Sem(E(nc.semaphore("gq")))
        sq = Sem(E(nc.semaphore("sq")))
        tq = Sem(E(nc.semaphore("tq")))
        stq = Sem(E(nc.semaphore("stq")))
        SEM = {"tensor": pe, "vector": ve, "scalar": sc}

        def mk(eng, meth, *a, inc=None, wait=(), **kw):
            def thunk(e, _a=a, _kw=kw, _m=meth, _w=tuple(wait), _i=inc):
                for s, v in _w:
                    e.wait_ge(s.h, v)
                ins = getattr(e, _m)(*_a, **_kw)
                if _i is not None:
                    ins.then_inc(_i[0].h, _i[1])
            P.q[eng].append(thunk)

        def op(eng, meth, *a, wait=(), **kw):
            s = SEM[eng]
            wait = ((s, s.n),) + tuple(wait)
            s.n += 1
            mk(eng, meth, *a, inc=(s, 1), wait=wait, **kw)
            return s.n

        def mm(*a, wait=(), **kw):
            mk("tensor", "matmul", *a, wait=wait, **kw)

        def mmi(*a, wait=(), **kw):
            pe.n += 1
            mk("tensor", "matmul", *a, inc=(pe, 1), wait=wait, **kw)
            return pe.n

        def tpi(*a, wait=(), **kw):
            pe.n += 1
            mk("tensor", "transpose", *a, inc=(pe, 1), wait=wait, **kw)
            return pe.n

        def dummy(n, b, rhs):
            for _ in range(n):
                mm(b, sb_id[:], rhs, start=True, stop=True)

        def dma(eng, dst, src, sem, wait=()):
            sem.n += 16
            mk(eng, "dma_start", dst, src, inc=(sem, 16), wait=wait)
            return sem.n

        def dmat(dst, src, wait=()):
            tq.n += 16
            mk("sync", "dma_start_transpose", dst, src, inc=(tq, 16), wait=wait)
            return tq.n

        def wt(eng, *pairs):
            P.q[eng].append(lambda e, _p=tuple(pairs): [e.wait_ge(s.h, v) for s, v in _p])

        sb_id = E(nc.sbuf_tensor("sb_id", [128, 128], BF16))
        sb_ones = E(nc.sbuf_tensor("sb_ones", [1, 1024], BF16))
        sb_qkvb = E(nc.sbuf_tensor("sb_qkvb", [128, 24], F32))
        sb_outb = E(nc.sbuf_tensor("sb_outb", [1, 1024], BF16))
        sb_xnTA = E(nc.sbuf_tensor("sb_xnTA", [128, 8 * HALF], BF16))
        sb_c8 = E(nc.sbuf_tensor("sb_c8", [8, HALF], BF16))

        for dst, src in ((sb_id[:], c_ident), (sb_ones[:], c_ones), (sb_qkvb[:], qkvb)):
            dma("sync", dst, src, sq)
        dma("gpsimd", sb_outb[:], outb, gq)
        CONST_GQ = gq.n

        with ExitStack() as STA:
            EA = STA.enter_context
            sb_relu = EA(nc.sbuf_tensor("sb_relu", [128, 4 * 512], F32))
            sb_qm = EA(nc.sbuf_tensor("sb_qm", [128, 130], BF16))
            sb_e2 = EA(nc.sbuf_tensor("sb_e2", [128, 2], BF16))
            sb_cv = EA(nc.sbuf_tensor("sb_cv", [128, CV_ARR.shape[1]], F32))
            sb_xnTB = EA(nc.sbuf_tensor("sb_xnTB", [128, 8 * HALF], BF16))
            sb_avp = EA(nc.sbuf_tensor("sb_avp", [128, NPAIR * HALF], BF16))
            sb_gwf = EA(nc.sbuf_tensor("sb_gwf", [128, 64], F32))
            sb_gwh = EA(nc.sbuf_tensor("sb_gwh", [128, 64], BF16))
            sb_gwl = EA(nc.sbuf_tensor("sb_gwl", [128, 64], BF16))
            dma("sync", sb_gwf[:], gwf, sq)
            for dst, src_ in ((sb_relu[:], c_relu), (sb_qm[:], c_qm), (sb_e2[:], c_e2),
                              (sb_cv[:], c_cv)):
                dma("sync", dst, src_, sq)
            GWQ = dma("sync", sb_gwf[:], gwf, sq)

            with ExitStack() as STP:
                EP = STP.enter_context
                sb_xt = EP(nc.sbuf_tensor("sb_xt", [128, 2 * 1024], F32))
                sb_xn = EP(nc.sbuf_tensor("sb_xn", [128, 2 * 1024], BF16))
                sb_lo = EP(nc.sbuf_tensor("sb_lo", [128, 1024], BF16))
                sb_hiT = EP(nc.sbuf_tensor("sb_hiT", [128, 2 * 1024], BF16))
                sb_loT = EP(nc.sbuf_tensor("sb_loT", [128, 2 * 1024], BF16))
                sb_st1 = EP(nc.sbuf_tensor("sb_st1", [128, 16], F32))
                sb_fsq = EP(nc.sbuf_tensor("sb_fsq", [128, 1024], F32))
                sb_r1 = EP(nc.sbuf_tensor("sb_r1", [128, 8], F32))
                sb_r2 = EP(nc.sbuf_tensor("sb_r2", [128, 8], F32))
                sb_r3 = EP(nc.sbuf_tensor("sb_r3", [128, 8], F32))
                sb_cmb = EP(nc.sbuf_tensor("sb_cmb", [128, 8], BF16))
                ps_r = [EP(nc.psum_tensor(f"ps_r{i}", [128, 8], F32)) for i in range(2)]
                ps_tr = EP(nc.psum_tensor("ps_tr", [8, 128], BF16))
                ps_t2 = [EP(nc.psum_tensor(f"ps_t2{i}", [128, 128], BF16))
                         for i in range(2)]
                ps_dm0 = EP(nc.psum_tensor("ps_dm0", [128, 512], F32))
                if "lg" in dbg_aps:
                    sb_lg = EP(nc.sbuf_tensor("sb_lg", [128, 16 * 8], F32))
                if "lg2" in dbg_aps:
                    sb_lg2 = EP(nc.sbuf_tensor("sb_lg2", [128, 16 * 8], F32))
                    ps_r2 = EP(nc.psum_tensor("ps_r2", [128, 8], F32))

                v = op("vector", "tensor_copy", sb_gwh[:], sb_gwf[:], wait=((sq, GWQ),))
                v = op("vector", "tensor_sub", sb_fsq[:, 0:64], sb_gwf[:], sb_gwh[:],
                       wait=((ve, v),))
                VGW = op("vector", "tensor_copy", sb_gwl[:], sb_fsq[:, 0:64])

                R3 = 1.0 / D
                LAST_TP = [0, 0]
                LAST_APPLY = [0, 0]
                LAST_XUSE = [0, 0]
                LOTP_PE = 0
                T2EV = [0, 0]
                LAST_MMR = 0
                LAST_T5 = 0
                ROUTER_V = 0
                for ti in range(32):
                    own = ti < 16
                    par = ti % 2
                    t0 = (ti % 16) * 128
                    xsrc = xo if own else xr
                    xtile = sb_xt[:, par * 1024:(par + 1) * 1024]
                    xntile = sb_xn[:, par * 1024:(par + 1) * 1024]
                    w_ld = ((ve, LAST_XUSE[par]),) if ti >= 2 else ()
                    dq = dma("sync", xtile, xsrc[t0:t0 + 128, :], sq, wait=w_ld)
                    v = op("vector", "reduce_sum", sb_st1[:, 0:1], xtile, AXX,
                           wait=((sq, dq),))
                    v = op("vector", "tensor_mul", sb_fsq[:], xtile, xtile)
                    v = op("vector", "reduce_sum", sb_st1[:, 1:2], sb_fsq[:], AXX)
                    v = op("vector", "tensor_scalar", sb_st1[:, 2:3], sb_st1[:, 0:1],
                           R3, None, ALU.mult)
                    v = op("vector", "tensor_scalar", sb_st1[:, 3:4], sb_st1[:, 1:2],
                           R3, None, ALU.mult)
                    v = op("vector", "tensor_mul", sb_st1[:, 4:5], sb_st1[:, 2:3],
                           sb_st1[:, 2:3], wait=((ve, v),))
                    v = op("vector", "tensor_sub", sb_st1[:, 5:6], sb_st1[:, 3:4],
                           sb_st1[:, 4:5], wait=((ve, v),))
                    v = op("vector", "tensor_scalar", sb_st1[:, 6:7], sb_st1[:, 5:6],
                           1e-5, None, ALU.add, wait=((ve, v),))
                    cl, ce = 7 + 4 * par, 8 + 4 * par
                    cl, ce = (7, 8) if par == 0 else (9, 10)
                    s1 = op("scalar", "activation", sb_st1[:, cl:cl + 1],
                            sb_st1[:, 6:7], AF.Ln,
                            wait=((ve, v),) + (((ve, LAST_APPLY[par]),) if ti >= 2 else ()))
                    s1 = op("scalar", "activation", sb_st1[:, ce:ce + 1],
                            sb_st1[:, cl:cl + 1], AF.Exp, scale=-0.5, wait=((sc, s1),))
                    wap = ((sc, s1), (ve, v))
                    if ti >= 2:
                        wap += ((tq, LAST_TP[par]),)
                    vap = op("vector", "tensor_scalar", xntile, xtile,
                             sb_st1[:, 2:3], sb_st1[:, ce:ce + 1], ALU.subtract,
                             ALU.mult, wait=wap)
                    LAST_APPLY[par] = vap
                    LAST_XUSE[par] = vap
                    dstT = (sb_xnTA if own else sb_xnTB)[:].rearrange(
                        "p (dc t) -> p dc t", dc=8)[:, :, t0:t0 + 128]
                    LAST_TP[par] = dmat(dstT, xntile.rearrange("p (dc f) -> p dc f", dc=8),
                                        wait=((ve, vap),))
                    if own:
                        v3 = op("vector", "tensor_scalar", sb_fsq[:], xtile,
                                sb_st1[:, 2:3], sb_st1[:, ce:ce + 1], ALU.subtract,
                                ALU.mult, wait=((sc, s1), (ve, vap)))
                        wlo = ((ve, vap),)
                        if LOTP_PE:
                            wlo += ((pe, LOTP_PE),)
                        v3 = op("vector", "tensor_sub", sb_lo[:], sb_fsq[:], xntile,
                                wait=wlo)
                        LAST_XUSE[par] = v3
                        for k in range(16):
                            b = k % 2
                            if k < 8:
                                src = xntile[:, k * 128:(k + 1) * 128]
                                dstT = sb_hiT[:, par * 1024 + k * 128:
                                              par * 1024 + (k + 1) * 128]
                                wtp = ((ve, vap),) if k == 0 else ()
                            else:
                                src = sb_lo[:, (k - 8) * 128:(k - 7) * 128]
                                dstT = sb_loT[:, par * 1024 + (k - 8) * 128:
                                              par * 1024 + (k - 7) * 128]
                                wtp = ((ve, v3),) if k == 8 else ()
                            if T2EV[b]:
                                wtp += ((ve, T2EV[b]),)
                            tpk = tpi(ps_t2[b][:], src, sb_id[:], wait=wtp)
                            T2EV[b] = op("vector", "tensor_copy", dstT,
                                         ps_t2[b][:], wait=((pe, tpk),))
                        LOTP_PE = tpk
                        bank = ps_r[(ti // 2) % 2]
                        w0 = ((ve, T2EV[0]), (ve, T2EV[1]), (ve, VGW))
                        if ti >= 4:
                            w0 += ((ve, ROUTER_V),)
                        for dc in range(8):
                            xs = sb_hiT[:, par * 1024 + dc * 128:
                                        par * 1024 + (dc + 1) * 128]
                            ls = sb_loT[:, par * 1024 + dc * 128:
                                        par * 1024 + (dc + 1) * 128]
                            mm(bank[:], xs, sb_gwh[:, dc * 8:(dc + 1) * 8],
                               start=(dc == 0), stop=False, wait=w0 if dc == 0 else ())
                            mm(bank[:], xs, sb_gwl[:, dc * 8:(dc + 1) * 8],
                               start=False, stop=False)
                            if dc < 7:
                                mm(bank[:], ls, sb_gwh[:, dc * 8:(dc + 1) * 8],
                                   start=False, stop=False)
                            else:
                                MMR = mmi(bank[:], ls, sb_gwh[:, dc * 8:(dc + 1) * 8],
                                          start=False, stop=True)
                                LAST_MMR = MMR
                        if "lg2" in dbg_aps:
                            for dc in range(8):
                                xs2 = sb_hiT[:, par * 1024 + dc * 128:
                                             par * 1024 + (dc + 1) * 128]
                                ls2 = sb_loT[:, par * 1024 + dc * 128:
                                             par * 1024 + (dc + 1) * 128]
                                w2 = ((ve, LG2V),) if dc == 0 and ti >= 1 else ()
                                mm(ps_r2[:], xs2, sb_gwh[:, dc * 8:(dc + 1) * 8],
                                   start=(dc == 0), stop=False, wait=w2)
                                mm(ps_r2[:], xs2, sb_gwl[:, dc * 8:(dc + 1) * 8],
                                   start=False, stop=False)
                                MM2 = mmi(ps_r2[:], ls2,
                                          sb_gwh[:, dc * 8:(dc + 1) * 8],
                                          start=False, stop=(dc == 7))
                            LG2V = op("vector", "tensor_copy",
                                      sb_lg2[:, (ti % 16) * 8:(ti % 16) * 8 + 8],
                                      ps_r2[:], wait=((pe, MM2),))
                            dma("sync", dbg_aps["lg2"][:, (ti % 16) * 8:
                                                       (ti % 16) * 8 + 8],
                                sb_lg2[:, (ti % 16) * 8:(ti % 16) * 8 + 8], stq,
                                wait=((ve, LG2V),))
                        if ti == 8 and "hit" in dbg_aps:
                            dma("sync", dbg_aps["hit"][:, 0:1024],
                                sb_hiT[:, par * 1024:(par + 1) * 1024], stq,
                                wait=((pe, MMR),))
                            dma("sync", dbg_aps["hit"][:, 1024:2048],
                                sb_loT[:, par * 1024:(par + 1) * 1024], stq)
                        if "lg" in dbg_aps:
                            vlg = op("vector", "tensor_copy",
                                     sb_lg[:, (ti % 16) * 8:(ti % 16) * 8 + 8],
                                     bank[:], wait=((pe, MMR),))
                            dma("sync", dbg_aps["lg"][:, (ti % 16) * 8:(ti % 16) * 8 + 8],
                                sb_lg[:, (ti % 16) * 8:(ti % 16) * 8 + 8], stq,
                                wait=((ve, vlg),))
                        v4 = op("vector", "tensor_reduce", sb_st1[:, 11:12], bank[:],
                                AXX, ALU.max, negate=True, wait=((pe, MMR),))
                        s2 = op("scalar", "activation", sb_r1[:], bank[:], AF.Exp,
                                bias=sb_st1[:, 11:12], wait=((ve, v4),))
                        v4 = op("vector", "reduce_sum", sb_st1[:, 12:13], sb_r1[:],
                                AXX, wait=((sc, s2),))
                        v4 = op("vector", "reciprocal", sb_st1[:, 13:14],
                                sb_st1[:, 12:13], wait=((ve, v4),))
                        v4 = op("vector", "tensor_scalar", sb_r2[:], sb_r1[:],
                                sb_st1[:, 13:14], None, ALU.mult, wait=((ve, v4),))
                        v4 = op("vector", "reduce_max", sb_st1[:, 14:15], sb_r2[:],
                                AXX, wait=((ve, v4),))
                        v4 = op("vector", "tensor_scalar", sb_r3[:], sb_r2[:],
                                sb_st1[:, 14:15], -10.0, ALU.is_ge, ALU.mult,
                                wait=((ve, v4),))
                        v4 = op("vector", "tensor_add", sb_r3[:], sb_r3[:], sb_r2[:],
                                wait=((ve, v4),))
                        v4 = op("vector", "reduce_max", sb_st1[:, 15:16], sb_r3[:],
                                AXX, wait=((ve, v4),))
                        v4 = op("vector", "tensor_add", sb_r1[:, 0:1], sb_st1[:, 14:15],
                                sb_st1[:, 15:16], wait=((ve, v4),))
                        v4 = op("vector", "tensor_scalar", sb_r1[:, 0:1], sb_r1[:, 0:1],
                                1e-20, None, ALU.add, wait=((ve, v4),))
                        v4 = op("vector", "reciprocal", sb_r1[:, 1:2], sb_r1[:, 0:1],
                                wait=((ve, v4),))
                        v4 = op("vector", "tensor_scalar", sb_r3[:], sb_r2[:],
                                sb_st1[:, 15:16], None, ALU.is_ge, wait=((ve, v4),))
                        v4 = op("vector", "tensor_scalar", sb_r2[:], sb_r2[:],
                                sb_r1[:, 1:2], 2.5, ALU.mult, ALU.mult,
                                wait=((ve, v4),))
                        wcm = ((ve, v4),)
                        if LAST_T5:
                            wcm += ((pe, LAST_T5),)
                        v4 = op("vector", "tensor_mul", sb_cmb[:], sb_r2[:], sb_r3[:],
                                wait=wcm)
                        t5 = op("tensor", "transpose", ps_tr[:], sb_cmb[:], sb_id[:],
                                wait=((ve, v4),))
                        LAST_T5 = t5
                        ROUTER_V = op("vector", "tensor_copy", sb_c8[:, t0:t0 + 128],
                                      ps_tr[:], wait=((pe, t5),))
                        dummy(8, ps_dm0[:], sb_xnTA[:, 0:512])
                    else:
                        dummy(10, ps_dm0[:], sb_xnTA[:, 0:512])
                CMB_DONE = ROUTER_V
                if "comb" in dbg_aps:
                    dma("sync", dbg_aps["comb"], sb_c8[:], stq, wait=((ve, ROUTER_V),))
                if "gwhl" in dbg_aps:
                    dma("sync", dbg_aps["gwhl"][:, 0:64], sb_gwh[:], stq,
                        wait=((ve, ROUTER_V),))
                    dma("sync", dbg_aps["gwhl"][:, 64:128], sb_gwl[:], stq)
            P0_SQ = sq.n
            XNT_TQ = tq.n
            if "xnT" in dbg_aps:
                dma("sync", dbg_aps["xnT"][0:128, :], sb_xnTA[:], stq,
                    wait=((sq, P0_SQ), (tq, XNT_TQ)))
                dma("sync", dbg_aps["xnT"][128:256, :], sb_xnTB[:], stq)

            NPAIR_RUN = {"p0": 0, "pair1": 1, "prep1": 1}.get(stage, NPAIR)
            DO_OUT = stage in ("att", "full")
            DO_MOE = stage == "full"

            with ExitStack() as STB:
                EB = STB.enter_context
                sb_kh = [EB(nc.sbuf_tensor(f"sb_kh{h}", [68, S], BF16)) for h in range(2)]
                sb_qh = [EB(nc.sbuf_tensor(f"sb_qh{h}", [68, HALF], BF16)) for h in range(2)]
                sb_vT = EB(nc.sbuf_tensor("sb_vT", [128, S], BF16))
                sb_vh = [EB(nc.sbuf_tensor(f"sb_vh{h}", [128, 32 * 80], BF16))
                         for h in range(2)]
                sb_wp = EB(nc.sbuf_tensor("sb_wp", [128, 2 * 3072], BF16))
                sb_ex = EB(nc.sbuf_tensor("sb_ex", [128, 4 * 512], BF16))
                sb_hop = EB(nc.sbuf_tensor("sb_hop", [128, 2 * 512], BF16))
                sb_sq2 = EB(nc.sbuf_tensor("sb_sq2", [128, 2 * 512], BF16))
                sb_ms = EB(nc.sbuf_tensor("sb_ms", [1, 2 * HALF], BF16))
                sb_mh = EB(nc.sbuf_tensor("sb_mh", [1, HALF], BF16))
                sb_km = EB(nc.sbuf_tensor("sb_km", [1, 24], F32))
                sb_rr = EB(nc.sbuf_tensor("sb_rr", [65, HALF], BF16))
                sb_rb0 = EB(nc.sbuf_tensor("sb_rb0", [1, HALF], BF16))

                MEMS_V = [0, 0]
                for h in range(2):
                    MEMS_V[h] = op("vector", "memset", sb_vh[h][:].rearrange(
                        "p (c w) -> p c w", w=80)[:, :, 64:65], 1.0)

                WPQ = [0, 0]
                WPQ[0] = dma("gpsimd",
                             sb_wp[:, 0:3072].rearrange("p (dc c) -> p dc c", dc=8),
                             warr[:, 0, :].rearrange("(dc p) c -> p dc c", p=128), gq,
                             wait=((gq, CONST_GQ),))

                FLASH_END = {"pe": pe.n, "ve": ve.n, "sc": sc.n}
                for pair in range(NPAIR_RUN):
                    par = pair % 2
                    wp0 = par * 3072
                    sl_pair = [0.5 ** (2 * pair), 0.5 ** (2 * pair + 1)]

                    if pair + 1 < NPAIR:
                        WPQ[1 - par] = dma(
                            "gpsimd",
                            sb_wp[:, (1 - par) * 3072:(2 - par) * 3072].rearrange(
                                "p (dc c) -> p dc c", dc=8),
                            warr[:, pair + 1, :].rearrange("(dc p) c -> p dc c", p=128),
                            gq, wait=((pe, FLASH_END["pe"]),) if pair >= 1 else ())
                    with ExitStack() as SPP:
                        EPP = SPP.enter_context
                        ps_pj = [EPP(nc.psum_tensor(f"pj{pair}_{i}", [128, 512], F32))
                                 for i in range(2)]
                        ps_q = EPP(nc.psum_tensor(f"pq{pair}", [1, 512], F32))
                        ps_k2 = EPP(nc.psum_tensor(f"pk2{pair}", [1, 512], F32))
                        ps_dm1 = EPP(nc.psum_tensor(f"pdm{pair}", [128, 512], F32))

                        wait_prev = ((sc, FLASH_END["sc"]), (ve, FLASH_END["ve"]),
                                     (pe, FLASH_END["pe"]))
                        KQC = 0
                        for h in range(2):
                            hh = 2 * pair + h
                            KQC = dma("sync", sb_kh[h][64:68, :],
                                      c_kc[(hh * 2) * 4:(hh * 2) * 4 + 4, :], sq,
                                      wait=((pe, FLASH_END["pe"]),))
                            KQC = dma("sync", sb_qh[h][64:67, :],
                                      c_qc[hh * 3:hh * 3 + 3, :], sq)

                        def rhs_xnT(dc, j0):
                            if j0 < HALF:
                                return sb_xnTA[:, dc * HALF + j0: dc * HALF + j0 + 512]
                            j0 -= HALF
                            return sb_xnTB[:, dc * HALF + j0: dc * HALF + j0 + 512]

                        HOP_Q = [0, 0]
                        BANK_FREE_V = [0, 0]

                        def proj_tiles(kind_off, nt, evac):
                            for t in range(nt):
                                bank = ps_pj[t % 2]
                                for dc in range(8):
                                    w = ()
                                    if dc == 0:
                                        w = ((gq, WPQ[par]), (sq, KQC),
                                             (tq, XNT_TQ)) + wait_prev
                                        if BANK_FREE_V[t % 2]:
                                            w += ((ve, BANK_FREE_V[t % 2]),)
                                    lhs = sb_wp[:, wp0 + dc * 384 + kind_off:
                                                wp0 + dc * 384 + kind_off + 128]
                                    if dc < 7:
                                        mm(bank[:], lhs, rhs_xnT(dc, t * 512),
                                           start=(dc == 0), stop=False, wait=w)
                                    else:
                                        MMX = mmi(bank[:], lhs, rhs_xnT(dc, t * 512),
                                                  start=False, stop=True, wait=w)
                                BANK_FREE_V[t % 2] = evac(t, bank, MMX)

                        def evac_k(t, bank, MMX):
                            bcol = sb_qkvb[:, 3 * pair + 1: 3 * pair + 2]
                            op("vector", "tensor_scalar",
                               sb_kh[0][0:64, t * 512:(t + 1) * 512],
                               bank[0:64, :], bcol[0:64, :], None, ALU.add,
                               wait=((pe, MMX),))
                            wB = ((sq, HOP_Q[t % 2]),) if HOP_Q[t % 2] else ()
                            vB = op("vector", "tensor_scalar",
                                    sb_hop[64:128, (t % 2) * 512:(t % 2 + 1) * 512],
                                    bank[64:128, :], bcol[64:128, :], None, ALU.add,
                                    wait=wB)
                            HOP_Q[t % 2] = dma(
                                "sync", sb_kh[1][0:64, t * 512:(t + 1) * 512],
                                sb_hop[64:128, (t % 2) * 512:(t % 2 + 1) * 512],
                                sq, wait=((ve, vB),))
                            return vB

                        proj_tiles(128, 8, evac_k)
                        HOPK_LAST = sq.n

                        def evac_v(t, bank, MMX):
                            bcol = sb_qkvb[:, 3 * pair + 2: 3 * pair + 3]
                            return op("vector", "tensor_scalar",
                                      sb_vT[:, t * 512:(t + 1) * 512], bank[:],
                                      bcol, None, ALU.add, wait=((pe, MMX),))

                        proj_tiles(256, 8, evac_v)
                        VEV = ve.n
                        VHQ = 0
                        for h in range(2):
                            wv = ((ve, VEV), (ve, MEMS_V[h]))
                            if pair >= 1:
                                wv += ((pe, FLASH_END["pe"]),)
                            VHQ = dmat(sb_vh[h][:].rearrange(
                                "p (c w) -> p c w", w=80)[:, :, 0:64],
                                sb_vT[h * 64:(h + 1) * 64, :].rearrange(
                                    "p (c f) -> p c f", c=32), wait=wv)

                        def evac_q(t, bank, MMX):
                            bcol = sb_qkvb[:, 3 * pair: 3 * pair + 1]
                            op("vector", "tensor_scalar",
                               sb_qh[0][0:64, t * 512:(t + 1) * 512],
                               bank[0:64, :], bcol[0:64, :], 0.125, ALU.add,
                               ALU.mult, wait=((pe, MMX),))
                            wB = ((sq, HOP_Q[t % 2]),)
                            vB = op("vector", "tensor_scalar",
                                    sb_hop[64:128, (t % 2) * 512:(t % 2 + 1) * 512],
                                    bank[64:128, :], bcol[64:128, :], 0.125, ALU.add,
                                    ALU.mult, wait=wB)
                            HOP_Q[t % 2] = dma(
                                "sync", sb_qh[1][0:64, t * 512:(t + 1) * 512],
                                sb_hop[64:128, (t % 2) * 512:(t % 2 + 1) * 512],
                                sq, wait=((ve, vB),))
                            return vB

                        proj_tiles(0, 4, evac_q)
                        HOPQ_LAST = sq.n
                        QKEV = ve.n

                        onescol = sb_qm[0:64, 64:65]
                        SQMM = [0, 0]
                        KMR = 0
                        for h in range(2):
                            src = sb_kh[h]
                            for t in range(8):
                                slot = t % 2
                                w = ((ve, QKEV),) if h == 0 else ((sq, HOPK_LAST),)
                                if SQMM[slot]:
                                    w += ((pe, SQMM[slot]),)
                                sS = op("scalar", "activation",
                                        sb_sq2[0:64, slot * 512:(slot + 1) * 512],
                                        src[0:64, t * 512:(t + 1) * 512],
                                        AF.Square, wait=w)
                                wm = ((sc, sS),)
                                if KMR:
                                    wm += ((ve, KMR),)
                                SQMM[slot] = mmi(ps_k2[0:1, :], onescol,
                                                 sb_sq2[0:64, slot * 512:(slot + 1) * 512],
                                                 start=True, stop=True, wait=wm)
                                KMR = op("vector", "reduce_max",
                                         sb_km[0:1, h * 8 + t: h * 8 + t + 1],
                                         ps_k2[0:1, :], AXX, wait=((pe, SQMM[slot]),))
                                dummy(2, ps_dm1[:], sb_xnTA[:, 0:512])
                        v = op("vector", "reduce_max", sb_km[0:1, 16:17],
                               sb_km[0:1, 0:8], AXX, wait=((ve, KMR),))
                        v = op("vector", "reduce_max", sb_km[0:1, 17:18],
                               sb_km[0:1, 8:16], AXX, wait=((ve, v),))
                        skm = op("scalar", "activation", sb_km[0:1, 18:20],
                                 sb_km[0:1, 16:18], AF.Ln, wait=((ve, v),))
                        KMBV = op("vector", "tensor_scalar", sb_km[0:1, 20:22],
                                  sb_km[0:1, 18:20], 0.5, MARGIN, ALU.mult, ALU.add,
                                  wait=((sc, skm),))
                        MH_Q = 0
                        for h in range(2):
                            LNQ = 0
                            for t in range(4):
                                slot = t % 2
                                w = ((ve, QKEV),) if h == 0 else ((sq, HOPQ_LAST),)
                                if SQMM[slot]:
                                    w += ((pe, SQMM[slot]),)
                                sS = op("scalar", "activation",
                                        sb_sq2[0:64, slot * 512:(slot + 1) * 512],
                                        sb_qh[h][0:64, t * 512:(t + 1) * 512],
                                        AF.Square, wait=w)
                                wm = ((sc, sS),)
                                if LNQ:
                                    wm += ((sc, LNQ),)
                                SQMM[slot] = mmi(ps_q[0:1, :], onescol,
                                                 sb_sq2[0:64, slot * 512:(slot + 1) * 512],
                                                 start=True, stop=True, wait=wm)
                                LNQ = op("scalar", "activation",
                                         sb_ms[0:1, h * HALF + t * 512:
                                               h * HALF + (t + 1) * 512],
                                         ps_q[0:1, :], AF.Ln, wait=((pe, SQMM[slot]),))
                                dummy(2, ps_dm1[:], sb_xnTA[:, 0:512])
                            w = ((sc, LNQ), (ve, KMBV))
                            if h == 1:
                                w += ((sq, MH_Q),)
                            sE = op("scalar", "activation", sb_mh[0:1, :],
                                    sb_ms[0:1, h * HALF:(h + 1) * HALF], AF.Exp,
                                    scale=0.5, bias=sb_km[0:1, 20 + h:21 + h], wait=w)
                            MH_Q = dma("sync", sb_qh[h][67:68, :], sb_mh[0:1, :],
                                       sq, wait=((sc, sE),))
                            dummy(4, ps_dm1[:], sb_xnTA[:, 0:512])
                        PREP = {"ve": ve.n, "sc": sc.n, "sq": sq.n, "pe": pe.n}

                    with ExitStack() as SPF:
                        EPF = SPF.enter_context
                        ps_sc = [EPF(nc.psum_tensor(f"sc{pair}_{i}", [128, 512], F32))
                                 for i in range(3)]
                        ps_av = [EPF(nc.psum_tensor(f"av{pair}_{i}", [65, 512], F32))
                                 for i in range(4)]
                        ps_bc = EPF(nc.psum_tensor(f"bc{pair}", [128, 512], F32))

                        AV_FREE = [FLASH_END["ve"]] * 4
                        BC_FREE = FLASH_END["ve"]
                        RB_Q = 0
                        LAST_MB = 0
                        scnt = 0
                        EXP_S = {}
                        AV_PE = {}
                        NH_RUN, NS_RUN = flash_lim if flash_lim else (2, 2)
                        if stage == "prep1":
                            NH_RUN = 0
                        for h in range(NH_RUN):
                            sl = sl_pair[h]
                            KEPT = [[], []]
                            for side_ in range(2):
                                for ig_ in range(4):
                                    io_ = ig_ * 512
                                    js = (range(0, 4 * ig_ + 4) if side_ == 0
                                          else range(4 * ig_ + 4, 32))
                                    KEPT[side_].append(
                                        [jc_ for jc_ in js
                                         if sl * max(0, io_ - (jc_ * 128 + 127),
                                                     jc_ * 128 - (io_ + 511)) < 110.0])
                            last_of = []
                            for ig_ in range(4):
                                if NS_RUN == 2 and KEPT[1][ig_]:
                                    last_of.append((1, KEPT[1][ig_][-1]))
                                else:
                                    last_of.append((0, KEPT[0][ig_][-1]))
                            SIDE_Q = 0
                            SKEW = 3
                            pend = []

                            def emit_av(sct, igx, jcx, first, last):
                                wav = ((sc, EXP_S[sct]), (tq, VHQ))
                                if first:
                                    wav += ((ve, AV_FREE[igx]),)
                                AV_PE[sct] = mmi(
                                    ps_av[igx][:],
                                    sb_vh[h][:, jcx * 80: jcx * 80 + 65],
                                    sb_ex[:, (sct % 4) * 512:(sct % 4 + 1) * 512],
                                    start=first, stop=last, wait=wav)
                            for side in range(NS_RUN):
                                if side == 1:
                                    hh = 2 * pair + h
                                    SIDE_Q = dma("sync", sb_kh[h][64:68, :],
                                                 c_kc[(hh * 2 + 1) * 4:
                                                      (hh * 2 + 1) * 4 + 4, :], sq,
                                                 wait=((pe, pe.n),))
                                for ig in range(4):
                                    io = ig * 512
                                    jcs = KEPT[side][ig]
                                    first_side1 = jcs[0] if jcs else -1
                                    for jc in jcs:
                                        jo = jc * 128
                                        sgn = -1.0 if side == 0 else 1.0
                                        Cb = sgn * sl * (io - jo + 192)
                                        bank = ps_sc[scnt % 3]
                                        w = ((sq, max(PREP["sq"], SIDE_Q)),
                                             (ve, PREP["ve"]), (sc, PREP["sc"]))
                                        if scnt >= 3:
                                            w = ((sc, EXP_S[scnt - 3]),)
                                            if side == 1 and jc == first_side1:
                                                w += ((sq, SIDE_Q),)
                                        MS = mmi(bank[:],
                                                 sb_kh[h][0:68, jo:jo + 128],
                                                 sb_qh[h][0:68, io:io + 512],
                                                 start=True, stop=True, wait=w)
                                        if len(pend) >= SKEW:
                                            emit_av(*pend.pop(0))
                                        diag = (side == 0 and 4 * ig <= jc <= 4 * ig + 3)
                                        wx = ((pe, MS),)
                                        if diag:
                                            dv = jc - 4 * ig
                                            vd = op("vector", "scalar_tensor_tensor",
                                                    bank[:],
                                                    sb_relu[:, dv * 512:(dv + 1) * 512],
                                                    -2.0 * sl, bank[:], ALU.mult,
                                                    ALU.add, wait=((pe, MS),))
                                            wx = ((ve, vd),)
                                        cvc = CV_MAP[float(np.float32(Cb))]
                                        sc.n += 1
                                        mk("scalar", "activation",
                                           sb_ex[:, (scnt % 4) * 512:(scnt % 4 + 1) * 512],
                                           bank[:], AF.Exp,
                                           bias=sb_cv[:, cvc:cvc + 1],
                                           inc=(sc, 1), wait=wx)
                                        EXP_S[scnt] = sc.n
                                        first = (side == 0 and jc == KEPT[0][ig][0])
                                        last = ((side, jc) == last_of[ig])
                                        pend.append((scnt, ig, jc, first, last))
                                        scnt += 1
                            while pend:
                                emit_av(*pend.pop(0))
                            HOPA = [0, 0]
                            RECV = 0
                            LAST_AV = AV_PE[scnt - 1]
                            for ig in range(4):
                                cols = slice(pair * HALF + ig * 512,
                                             pair * HALF + (ig + 1) * 512)
                                if h == 0:
                                    vE = op("vector", "tensor_copy",
                                            sb_avp[0:64, cols], ps_av[ig][0:64, :],
                                            wait=((pe, LAST_AV),))
                                else:
                                    w = ((pe, LAST_AV),)
                                    if ig >= 2:
                                        w += ((sq, HOPA[ig % 2]),)
                                    vE = op("vector", "tensor_copy",
                                            sb_hop[0:64, (ig % 2) * 512:(ig % 2 + 1) * 512],
                                            ps_av[ig][0:64, :], wait=w)
                                    HOPA[ig % 2] = dma("sync", sb_avp[64:128, cols],
                                                       sb_hop[0:64, (ig % 2) * 512:
                                                              (ig % 2 + 1) * 512],
                                                       sq, wait=((ve, vE),))
                                wrc = ((pe, LAST_AV),)
                                if h == 1:
                                    wrc += ((sq, RB_Q),)
                                RECV = op("vector", "reciprocal",
                                          sb_rr[64:65, ig * 512:(ig + 1) * 512],
                                          ps_av[ig][64:65, :], wait=wrc)
                                AV_FREE[ig] = max(vE, RECV)
                                dummy(2, ps_sc[0][:], sb_xnTA[:, 0:512])
                            wrb = ((ve, RECV),)
                            if h == 1:
                                wrb += ((pe, LAST_MB),)
                            RB_Q = dma("sync", sb_rb0[:], sb_rr[64:65, :], sq,
                                       wait=wrb)
                            for ig in range(4):
                                wbc = ((sq, RB_Q), (ve, BC_FREE))
                                tp = (0, 64 * h)
                                MB = mmi(ps_bc[64 * h:64 * h + 64, :],
                                         sb_ones[0:1, 0:64],
                                         sb_rb0[0:1, ig * 512:(ig + 1) * 512],
                                         start=True, stop=True, tile_position=tp,
                                         wait=wbc)
                                LAST_MB = MB
                                cols = slice(pair * HALF + ig * 512,
                                             pair * HALF + (ig + 1) * 512)
                                w = ((pe, MB),)
                                if h == 1:
                                    w += ((sq, HOPA[ig % 2]),)
                                BC_FREE = op("vector", "tensor_tensor",
                                             sb_avp[64 * h:64 * h + 64, cols],
                                             sb_avp[64 * h:64 * h + 64, cols],
                                             ps_bc[64 * h:64 * h + 64, :], ALU.mult,
                                             wait=w)
                                dummy(2, ps_sc[0][:], sb_xnTA[:, 0:512])
                        FLASH_END = {"pe": pe.n, "ve": ve.n, "sc": sc.n}
                AVP_DONE = {"ve": ve.n, "sq": sq.n, "pe": pe.n, "sc": sc.n}

            if "avp" in dbg_aps:
                dma("sync", dbg_aps["avp"], sb_avp[:], stq,
                    wait=((ve, AVP_DONE["ve"]), (sq, AVP_DONE["sq"])))

            sb_s = ST.enter_context(
                nc.sbuf_tensor("sb_s", [128, 16 * 1024], BF16, side="right"))
            with ExitStack() as STO:
                EO = STO.enter_context
                sb_ow = EO(nc.sbuf_tensor("sb_ow", [128, 8 * 1024], BF16))
                ps_o = [EO(nc.psum_tensor(f"ps_o{i}", [128, 512], F32))
                        for i in range(2)]
                OWQ = dma("gpsimd", sb_ow[:].rearrange("p (dc n) -> p dc n", dc=8),
                          owT.rearrange("(dc p) n -> p dc n", p=128), gq,
                          wait=((pe, AVP_DONE["pe"]), (sc, AVP_DONE["sc"]),
                                (ve, AVP_DONE["ve"]), (sq, AVP_DONE["sq"])))
                for tt in range(16 if DO_OUT else 0):
                    for nn in range(2):
                        bank = ps_o[(tt * 2 + nn) % 2]
                        w0 = ((gq, OWQ), (ve, AVP_DONE["ve"]), (sq, AVP_DONE["sq"]))
                        if tt * 2 + nn >= 2:
                            w0 = ((ve, SFREE[(tt * 2 + nn) % 2]),)
                        for p in range(8):
                            mm(bank[:], sb_avp[:, p * HALF + tt * 128:
                                               p * HALF + tt * 128 + 128],
                               sb_ow[:, p * 1024 + nn * 512: p * 1024 + nn * 512 + 512],
                               start=(p == 0), stop=False, wait=w0 if p == 0 else ())
                        MO = mmi(bank[:], sb_ones[0:1, 0:128],
                                 sb_outb[0:1, nn * 512:(nn + 1) * 512],
                                 start=False, stop=True)
                        SFREE = SFREE if tt * 2 + nn >= 2 else [0, 0]
                        SFREE[(tt * 2 + nn) % 2] = op(
                            "vector", "tensor_copy",
                            sb_s[:, tt * 1024 + nn * 512: tt * 1024 + nn * 512 + 512],
                            bank[:], wait=((pe, MO),))
                S_DONE = ve.n
            if "s" in dbg_aps:
                dma("sync", dbg_aps["s"], sb_s[:], stq, wait=((ve, S_DONE),))

        with ExitStack() as STM:
            EM = STM.enter_context
            sb_w = [EM(nc.sbuf_tensor(f"sb_w{i}", [128, 11264], BF16)) for i in range(2)]
            sb_gr = EM(nc.sbuf_tensor("sb_gr", [128, 2 * 11 * 512], BF16))
            sb_h = EM(nc.sbuf_tensor("sb_h", [128, 2 * 11 * 512], BF16))
            sb_cb = EM(nc.sbuf_tensor("sb_cb", [128, 1024], BF16))
            sb_cbr = EM(nc.sbuf_tensor("sb_cbr", [1, 1024], BF16))
            sb_y = EM(nc.sbuf_tensor("sb_y", [128, 8 * 1024], F32))
            sb_xs = EM(nc.sbuf_tensor("sb_xs", [128, 1024], F32))
            ps_g = [EM(nc.psum_tensor(f"ps_g{i}", [128, 512], F32)) for i in range(2)]
            ps_d = [EM(nc.psum_tensor(f"ps_d{i}", [128, 512], F32)) for i in range(2)]
            ps_cb = EM(nc.psum_tensor("ps_cb", [128, 512], F32))

            STQ_TILES = 0
            OP_PE = pe.n
            WFREE = [OP_PE, OP_PE]
            LQ = {"wg": 0, "wu": 0, "wd": 0, "cb": S_DONE}
            PREV_HEV = [S_DONE, S_DONE]
            PREV_YV = [S_DONE, S_DONE]
            LAST_MCB = OP_PE
            for half in range(2 if DO_MOE else 0):
                hoff = half * 1024
                for e in range(NE):
                    gbuf, ubuf = e % 2, 1 - e % 2
                    wgq = dma("gpsimd", sb_w[gbuf][:].rearrange(
                        "p (dc f) -> p dc f", dc=8),
                        wg_all[e].rearrange("(dc p) f -> p dc f", p=128), gq,
                        wait=((pe, WFREE[gbuf]),))
                    if e < 8:
                        cbq = dma("sync", sb_cbr[:],
                                  sb_c8[e:e + 1, hoff:hoff + 1024], sq,
                                  wait=((pe, LAST_MCB),))
                        for half128 in range(2):
                            for nck in range(2):
                                MCB = mmi(ps_cb[64 * half128:64 * half128 + 64, :],
                                          sb_ones[0:1, 0:64],
                                          sb_cbr[0:1, nck * 512:(nck + 1) * 512],
                                          start=True, stop=True,
                                          tile_position=(0, 64 * half128),
                                          wait=((sq, cbq), (ve, LQ["cb"])))
                                LAST_MCB = MCB
                                LQ["cb"] = op("vector", "tensor_copy",
                                              sb_cb[64 * half128:64 * half128 + 64,
                                                    nck * 512:(nck + 1) * 512],
                                              ps_cb[64 * half128:64 * half128 + 64, :],
                                              wait=((pe, MCB),))
                    GFREE = [ve.n, sc.n]
                    SIL = {}
                    for tc in range(2):
                        for fc in range(NFC):
                            k = tc * NFC + fc
                            bank = ps_g[k % 2]
                            for dc in range(8):
                                w = ()
                                if dc == 0:
                                    w = ((gq, wgq),)
                                    if k >= 2:
                                        w += ((sc, SIL[k - 2]),)
                                    else:
                                        w += ((ve, PREV_HEV[k]),)
                                lhs = sb_w[gbuf][:, dc * FE + fc * 128:
                                                 dc * FE + fc * 128 + 128]
                                rhs_ = sb_xnTA[:, dc * HALF + hoff + tc * 512:
                                               dc * HALF + hoff + tc * 512 + 512]
                                if dc < 7:
                                    mm(bank[:], lhs, rhs_, start=(dc == 0),
                                       stop=False, wait=w)
                                else:
                                    MMG = mmi(bank[:], lhs, rhs_, start=False,
                                              stop=True, wait=w)
                            SIL[k] = op("scalar", "activation",
                                        sb_gr[:, k * 512:(k + 1) * 512], bank[:],
                                        AF.Silu, wait=((pe, MMG),))
                    GATE_PE = pe.n
                    wuq = dma("gpsimd", sb_w[ubuf][:].rearrange(
                        "p (dc f) -> p dc f", dc=8),
                        wu_all[e].rearrange("(dc p) f -> p dc f", p=128), gq,
                        wait=((pe, WFREE[ubuf]),))
                    HEV = {}
                    for tc in range(2):
                        for fc in range(NFC):
                            k = tc * NFC + fc
                            bank = ps_g[k % 2]
                            for dc in range(8):
                                w = ()
                                if dc == 0:
                                    w = ((gq, wuq),)
                                    if k >= 2:
                                        w += ((ve, HEV[k - 2]),)
                                    else:
                                        w += ((sc, SIL[20 + k]),)
                                lhs = sb_w[ubuf][:, dc * FE + fc * 128:
                                                 dc * FE + fc * 128 + 128]
                                rhs_ = sb_xnTA[:, dc * HALF + hoff + tc * 512:
                                               dc * HALF + hoff + tc * 512 + 512]
                                if dc < 7:
                                    mm(bank[:], lhs, rhs_, start=(dc == 0),
                                       stop=False, wait=w)
                                else:
                                    MMU = mmi(bank[:], lhs, rhs_, start=False,
                                              stop=True, wait=w)
                            v = op("vector", "tensor_tensor",
                                   sb_h[:, k * 512:(k + 1) * 512],
                                   sb_gr[:, k * 512:(k + 1) * 512], bank[:],
                                   ALU.mult, wait=((pe, MMU), (sc, SIL[k])))
                            if e < 8:
                                v = op("vector", "tensor_tensor",
                                       sb_h[:, k * 512:(k + 1) * 512],
                                       sb_h[:, k * 512:(k + 1) * 512],
                                       sb_cb[:, tc * 512:(tc + 1) * 512], ALU.mult,
                                       wait=((ve, LQ["cb"]),))
                            HEV[k] = v
                    UP_PE = pe.n
                    wdq = dma("gpsimd", sb_w[gbuf][:].rearrange(
                        "p (fc d) -> p fc d", fc=NFC),
                        wd_all[e].rearrange("(fc p) d -> p fc d", p=128), gq,
                        wait=((pe, GATE_PE),))
                    YV = {}
                    dn = 0
                    for tc in range(2):
                        for ts in range(4):
                            for dsl in range(2):
                                bank = ps_d[dn % 2]
                                for fc in range(NFC):
                                    w = ()
                                    if fc == 0:
                                        w = ((gq, wdq), (ve, HEV[tc * NFC + NFC - 1]))
                                        if dn >= 2:
                                            w += ((ve, YV[dn - 2]),)
                                        else:
                                            w += ((ve, PREV_YV[dn]),)
                                    lhs = sb_h[:, (tc * NFC + fc) * 512 + ts * 128:
                                               (tc * NFC + fc) * 512 + ts * 128 + 128]
                                    rhs_ = sb_w[gbuf][:, fc * 1024 + dsl * 512:
                                                      fc * 1024 + dsl * 512 + 512]
                                    if fc < NFC - 1:
                                        mm(bank[:], lhs, rhs_, start=(fc == 0),
                                           stop=False, wait=w)
                                    else:
                                        MMD = mmi(bank[:], lhs, rhs_, start=False,
                                                  stop=True, wait=w)
                                ycols = slice((tc * 4 + ts) * 1024 + dsl * 512,
                                              (tc * 4 + ts) * 1024 + dsl * 512 + 512)
                                if e == 0:
                                    YV[dn] = op("vector", "tensor_copy",
                                                sb_y[:, ycols], bank[:],
                                                wait=((pe, MMD),))
                                else:
                                    YV[dn] = op("vector", "tensor_tensor",
                                                sb_y[:, ycols], sb_y[:, ycols],
                                                bank[:], ALU.add, wait=((pe, MMD),))
                                dn += 1
                    WFREE[gbuf] = pe.n
                    WFREE[ubuf] = UP_PE
                    PREV_HEV = [HEV[20], HEV[21]]
                    PREV_YV = [YV[14], YV[15]]
                    Y_DONE = ve.n
                if "y" in dbg_aps:
                    ytap = dma("sync", dbg_aps["y"][:, half * 8192:(half + 1) * 8192],
                               sb_y[:], stq, wait=((ve, Y_DONE),))
                    wt("scalar", (stq, ytap))
                XS_FREE = 0
                for tt in range(8):
                    ycols = slice(tt * 1024, (tt + 1) * 1024)
                    sg = op("scalar", "activation", sb_gr[:, 0:1024],
                            sb_y[:, ycols], AF.Silu, wait=((ve, Y_DONE),))
                    xq = dma("sync", sb_xs[:], xo[hoff + tt * 128: hoff + tt * 128 + 128, :],
                             sq, wait=((ve, XS_FREE),))
                    v = op("vector", "tensor_tensor", sb_y[:, ycols],
                           sb_s[:, (half * 8 + tt) * 1024:(half * 8 + tt + 1) * 1024],
                           sb_gr[:, 0:1024], ALU.mult, wait=((sc, sg),))
                    XS_FREE = op("vector", "tensor_add", sb_y[:, ycols],
                                 sb_y[:, ycols], sb_xs[:], wait=((ve, v), (sq, xq)))
                    STQ_TILES = dma("sync",
                                    out[hoff + tt * 128: hoff + tt * 128 + 128, :],
                                    sb_y[:, ycols], stq, wait=((ve, XS_FREE),))
                    Y_DONE = max(Y_DONE, XS_FREE)
        wt("sync", (stq, stq.n))

        def replay(q):
            def go(e):
                with nc.allow_low_precision(reason="bf16 matmul operands by design"):
                    for f in q:
                        f(e)
            return go

        with nc.Block() as block:
            block.gpsimd(replay(P.q["gpsimd"]))
            block.scalar(replay(P.q["scalar"]))
            block.tensor(replay(P.q["tensor"]))
            block.vector(replay(P.q["vector"]))
            block.sync(replay(P.q["sync"]))
    return nc


_NC_CACHE = {}


def get_nc(dbg=()):
    key = tuple(sorted(dbg))
    if key not in _NC_CACHE:
        _NC_CACHE[key] = build(dbg=key)
    return _NC_CACHE[key]


def make_inmaps(inputs):
    L = host_layouts(inputs)
    x = np.asarray(inputs["L0_attn_x"])
    shared = {k: L[k] for k in ("warr", "qkvb", "owT", "outb", "gwf",
                                "wg_all", "wu_all", "wd_all")}
    in_maps = []
    for core in range(8):
        b, half = core // 2, core % 2
        if half == 0:
            xo_ = x[b, 0:HALF]
            xr_ = x[b, HALF:S]
        else:
            xo_ = x[b, S - 1:HALF - 1:-1]
            xr_ = x[b, HALF - 1::-1]
        m = dict(shared)
        m["xo"] = np.ascontiguousarray(xo_)
        m["xr"] = np.ascontiguousarray(xr_)
        in_maps.append(m)
    return in_maps


def kernel(**inputs):
    nc = get_nc()
    in_maps = make_inmaps(inputs)
    res = run_bass_kernel_spmd(nc, in_maps, core_ids=list(range(8)))
    out = np.empty((4, S, D), dtype=np.float32)
    for core in range(8):
        b, half = core // 2, core % 2
        o = res.results[core]["o"]
        if half == 0:
            out[b, 0:HALF] = o
        else:
            out[b, HALF:S] = o[::-1]
    return out
